# version 02
# baseline (speedup 1.0000x reference)
import numpy as np
import ml_dtypes
import concourse.bass as bass
import concourse.mybir as mybir
from concourse.bass_utils import run_bass_kernel_spmd

F32 = mybir.dt.float32
BF16 = mybir.dt.bfloat16
AF = mybir.ActivationFunctionType
ALU = mybir.AluOpType
AX = mybir.AxisListType

B_FULL, T, C = 16, 2048, 1024
NCORES = 8
BPC = B_FULL // NCORES
NT = T // 128
NC_ = C // 128
H, HD = 16, 64
EPS = 1e-5
DFF = 4096
NEXP = 4
QB = 256
VS = 68

ENGINES = ("sync", "gpsimd", "tensor", "vector", "scalar")
STAGES = 4
DBG = ""


class Prog:
    def __init__(self, nc):
        self.nc = nc
        self.streams = {e: [] for e in ENGINES}
        self.sems = {}
        self.counts = {}
        self.waited = {e: {} for e in ENGINES}

    def sem(self, name):
        s = self.nc.semaphore(name).__enter__()
        self.sems[name] = s
        self.counts[name] = 0

    def emit(self, eng, fn):
        self.streams[eng].append(fn)

    def dma(self, eng, out, in_, sem_name, **kw):
        if sem_name not in self.sems:
            self.sem(sem_name)
        s = self.sems[sem_name]
        self.counts[sem_name] += 16
        self.emit(eng, lambda e: e.dma_start(out=out, in_=in_, **kw).then_inc(s, 16))

    def op(self, eng, method, *args, inc=None, **kw):
        if inc is None:
            self.emit(eng, lambda e: getattr(e, method)(*args, **kw))
        else:
            s = self.sems[inc]
            self.counts[inc] += 1
            self.emit(eng, lambda e: getattr(e, method)(*args, **kw).then_inc(s, 1))

    def wait(self, eng, sem_name, val=None):
        if sem_name not in self.counts:
            return
        if val is None:
            val = self.counts[sem_name]
        if val <= 0:
            return
        if val <= self.waited[eng].get(sem_name, 0):
            return
        self.waited[eng][sem_name] = val
        s = self.sems[sem_name]
        self.emit(eng, lambda e: e.wait_ge(s, val))

    def barrier(self):
        for e in ENGINES:
            s = self.sems["b_" + e]
            self.counts["b_" + e] += 1
            self.emit(e, lambda en, s=s: en.sem_inc(s, 1))
        for e in ENGINES:
            for o in ENGINES:
                self.wait(e, "b_" + o)


class Guard:
    def __init__(self, p, n):
        self.p = p
        self.g = [None] * n

    def acquire(self, i, eng):
        if self.g[i] is not None:
            self.p.wait(eng, self.g[i][0], self.g[i][1])

    def release(self, i, sem_name):
        self.g[i] = (sem_name, self.p.counts[sem_name])


def build_program(detect_races=True):
    nc = bass.Bass("TRN2", target_bir_lowering=False,
                   detect_race_conditions=detect_races)
    p = Prog(nc)
    for name in ("qS", "qG", "tens", "vect", "scal", "pool"):
        p.sem(name)
    for e in ENGINES:
        p.sem("b_" + e)

    dt_f = lambda n, sh: nc.dram_tensor(n, list(sh), F32, kind="ExternalInput").ap()
    dt_b = lambda n, sh: nc.dram_tensor(n, list(sh), BF16, kind="ExternalInput").ap()

    x_in = dt_f("x_in", (BPC * T, C))
    Wq = dt_f("Wq", (C, C)); bq = dt_f("bq", (C,))
    Wk = dt_f("Wk", (C, C)); bk = dt_f("bk", (C,))
    Wv = dt_f("Wv", (C, C)); bv = dt_f("bv", (C,))
    Wo = dt_f("Wo", (C, C)); bo = dt_f("bo", (C,))
    A_log = dt_f("A_log", (C, 16))
    W_x = dt_f("W_x", (96, C))
    W_dt = dt_f("W_dt", (C, 64))
    b_dt = dt_f("b_dt", (C,))
    D_par = dt_f("D_par", (C,))
    ln1_w = dt_f("ln1_w", (C,)); ln1_b = dt_f("ln1_b", (C,))
    ln2_w = dt_f("ln2_w", (C,)); ln2_b = dt_f("ln2_b", (C,))
    conv_w = dt_f("conv_w", (C, 3))
    attn_w = dt_f("attn_w", (C, 3 * C))
    proj_w = dt_f("proj_w", (C, C))
    router_w = dt_f("router_w", (C, NEXP))
    exp_wv = dt_f("exp_wv", (NEXP * C, 2 * DFF))
    exp_proj = dt_f("exp_proj", (NEXP * DFF, C))
    ident_f = dt_f("ident_f", (128, 128))
    ident_b = dt_b("ident_b", (128, 128))
    rmat = dt_b("rmat", (128, 128))
    cos_dm_in = dt_b("cos_dm_i", (128, T))
    sin_dm_in = dt_b("sin_dm_i", (128, T))
    masks_in = dt_b("masks", (128, 2 * QB))

    x_out = nc.dram_tensor("x_out", [BPC * T, C], F32, kind="ExternalOutput").ap()

    dram_i = lambda n, sh, dt: nc.dram_tensor(n, list(sh), dt, kind="Internal").ap()
    wq_bf = dram_i("wq_bf", (C, C), BF16)
    wk_bf = dram_i("wk_bf", (C, C), BF16)
    wv_bf = dram_i("wv_bf", (C, C), BF16)
    wo_bf = dram_i("wo_bf", (C, C), BF16)
    attn_w_bf = dram_i("attn_w_bf", (C, 3 * C), BF16)
    proj_w_bf = dram_i("proj_w_bf", (C, C), BF16)
    exp_wv_bf = dram_i("exp_wv_bf", (NEXP * C, 2 * DFF), BF16)
    exp_proj_bf = dram_i("exp_proj_bf", (NEXP * DFF, C), BF16)
    park_x = dram_i("park_x", (T, C), F32)
    bc_rows = dram_i("bc_rows", (32, T), F32)

    from contextlib import ExitStack
    es = ExitStack()
    sb = lambda n, sh, dt: es.enter_context(nc.sbuf_tensor(n, list(sh), dt))
    XA = sb("XA", (128, NT * 1024), F32)
    XB = sb("XB", (128, NT * 1024), F32)
    BIG = sb("BIG", (128, 9728), F32)
    POOL = sb("POOL", (128, 2048), F32)
    MISC = sb("MISC", (128, 3072), F32)
    idf = sb("idf", (128, 128), F32)
    idb = sb("idb", (128, 128), BF16)
    rmt = sb("rmt", (128, 128), BF16)
    cos_dm = sb("cos_dm", (128, T), BF16)
    sin_dm = sb("sin_dm", (128, T), BF16)
    stat = sb("stat", (128, 384), F32)
    smallv = sb("smallv", (128, 256), F32)
    wxt = sb("wxt", (128, 768), F32)
    wdtt = sb("wdtt", (64, 1024), F32)
    cmask = sb("cmask", (128, 2 * QB), BF16)
    carry = sb("carry", (128, 128), F32)
    logit = sb("logit", (128, 64), F32)
    emask = sb("emask", (128, 64), F32)
    rw = sb("rw", (128, 32), F32)
    P = [es.enter_context(nc.psum_tensor("P0", [128, 1024], F32)),
         es.enter_context(nc.psum_tensor("P1", [128, 1024], F32)),
         es.enter_context(nc.psum_tensor("P2", [128, 512], F32)),
         es.enter_context(nc.psum_tensor("P3", [128, 1024], F32))]
    PT = es.enter_context(nc.psum_tensor("PT", [128, 1024], BF16))

    M_BCA = MISC[:, 0:1024]
    M_BCB = MISC[:, 1024:2048]
    M_SS = MISC[:, 2048:3072]

    bitb = lambda ap: ap.bitcast(BF16)
    pool_bf = bitb(POOL)
    big_bf = bitb(BIG)

    CM = big_bf[:, 0:NC_ * T]
    SCR = BIG[:, 8192:9728]
    BNC = big_bf[:, 18432:19456]

    slots = Guard(p, 8)

    def punit(u):
        return pool_bf[:, 512 * u:512 * (u + 1)]

    def qk_slot(sset, kc):
        return pool_bf[:, 128 * (8 * sset + kc):128 * (8 * sset + kc) + 128]

    def v_slot(kc):
        return pool_bf[:, 2048 + 128 * kc:2048 + 128 * (kc + 1)]

    def op_slot(j):
        return pool_bf[:, 3072 + 512 * j:3072 + 512 * (j + 1)]

    def load_units(units, loads):
        for u in units:
            slots.acquire(u, "sync")
        sem = f"pu{units[0]}"
        for dst, srcap in loads:
            p.dma("sync", dst, srcap, sem)
        return sem

    def release_units(units, sem_name):
        for u in units:
            slots.release(u, sem_name)

    pguard = Guard(p, 8)

    def phalf(i, j):
        return P[i][:, 512 * j:512 * (j + 1)]

    dumped = [False]

    def dump_bf(ap_bf, ncols):
        if dumped[0]:
            return
        dumped[0] = True
        p.barrier()
        nblk = ncols // 1024
        for i in range(nblk):
            p.op("scalar", "copy", SCR[:, 0:1024], ap_bf[:, 1024 * i:1024 * (i + 1)],
                 inc="scal")
            p.wait("gpsimd", "scal")
            p.dma("gpsimd", x_out[128 * i:128 * (i + 1), :], SCR[:, 0:1024], "qO")
            p.wait("scalar", "qO")
        p.barrier()

    def load_consts():
        p.dma("sync", idf[:], ident_f, "qC")
        p.dma("sync", idb[:], ident_b, "qC")
        p.dma("sync", rmt[:], rmat, "qC")
        p.dma("sync", cos_dm[:], cos_dm_in, "qC")
        p.dma("sync", sin_dm[:], sin_dm_in, "qC")
        p.dma("sync", cmask[:], masks_in, "qC")
        for cc in range(NC_):
            sl = slice(128 * cc, 128 * (cc + 1))
            p.dma("sync", smallv[:, cc:cc + 1], bq[sl].rearrange("(p o) -> p o", o=1), "qC")
            p.dma("sync", smallv[:, 8 + cc:9 + cc], bk[sl].rearrange("(p o) -> p o", o=1), "qC")
            p.dma("sync", smallv[:, 16 + cc:17 + cc], b_dt[sl].rearrange("(p o) -> p o", o=1), "qC")
            p.dma("sync", smallv[:, 24 + 16 * cc:24 + 16 * (cc + 1)], A_log[sl, :], "qC")
            p.dma("sync", smallv[:, 160 + 3 * cc:163 + 3 * cc], conv_w[sl, :], "qC")
        p.wait("scalar", "qC")
        p.op("scalar", "activation", smallv[:, 24:152], smallv[:, 24:152], AF.Exp)
        p.op("scalar", "mul", smallv[:, 24:152], smallv[:, 24:152], -1.0, inc="scal")

    def prologue_weights():
        jobs = [(Wq, wq_bf, C, C), (Wk, wk_bf, C, C), (Wv, wv_bf, C, C),
                (Wo, wo_bf, C, C), (attn_w, attn_w_bf, C, 3 * C),
                (proj_w, proj_w_bf, C, C),
                (exp_wv, exp_wv_bf, NEXP * C, 2 * DFF),
                (exp_proj, exp_proj_bf, NEXP * DFF, C)]
        inb = Guard(p, 2)
        outb = Guard(p, 2)
        nb = 0
        for src, dst, rows, cols in jobs:
            for r in range(rows // 128):
                for c0 in range(0, cols, 1024):
                    w = min(1024, cols - c0)
                    i = nb % 2
                    buf = POOL[:, 1024 * i:1024 * i + w]
                    ob = big_bf[:, 16384 + 1024 * i:16384 + 1024 * i + w]
                    inb.acquire(i, "sync")
                    p.dma("sync", buf, src[128 * r:128 * (r + 1), c0:c0 + w], f"pi{i}")
                    outb.acquire(i, "vector")
                    p.wait("vector", f"pi{i}")
                    p.op("vector", "tensor_copy", ob, buf, inc="vect")
                    inb.release(i, "vect")
                    p.wait("gpsimd", "vect")
                    p.dma("gpsimd", dst[128 * r:128 * (r + 1), c0:c0 + w], ob, f"po{i}")
                    outb.release(i, f"po{i}")
                    nb += 1
        WXS = BIG[:, 0:1024]
        p.wait("vector", "po0")
        p.wait("vector", "po1")
        p.op("vector", "memset", WXS, 0.0, inc="vect")
        p.wait("sync", "vect")
        p.dma("sync", WXS[0:96, :], W_x, "qW")
        p.wait("tensor", "qW")
        for cc in range(NC_):
            p.op("tensor", "transpose", P[0][:, 0:128], WXS[:, 128 * cc:128 * (cc + 1)],
                 idf[:], inc="tens")
            p.wait("scalar", "tens")
            p.op("scalar", "copy", wxt[:, 96 * cc:96 * (cc + 1)], P[0][:, 0:96], inc="scal")
            p.wait("tensor", "scal")
        p.wait("sync", "scal")
        p.wait("sync", "qW")
        for cc in range(NC_):
            p.dma("sync", BIG[:, 1024 + 64 * cc:1024 + 64 * (cc + 1)],
                  W_dt[128 * cc:128 * (cc + 1), :], "qW")
        p.wait("tensor", "qW")
        for cc in range(NC_):
            p.op("tensor", "transpose", P[0][0:64, 128:256],
                 BIG[:, 1024 + 64 * cc:1024 + 64 * (cc + 1)], idf[:], inc="tens")
            p.wait("scalar", "tens")
            p.op("scalar", "copy", wdtt[:, 128 * cc:128 * (cc + 1)], P[0][0:64, 128:256],
                 inc="scal")
            p.wait("tensor", "scal")
        p.barrier()

    def dma_bcast(eng, dst_tile, src_row_ap, q):
        p.dma(eng, dst_tile, src_row_ap.partition_broadcast(128), q)

    def row(v):
        return v.rearrange("(o c) -> o c", o=1)

    def load_x(b, XT):
        p.wait("sync", "qO")
        for i in range(NT):
            p.dma("sync", XT[:, 1024 * i:1024 * (i + 1)],
                  x_in[b * T + 128 * i: b * T + 128 * (i + 1), :], "qX")

    def store_x(b, XT):
        p.wait("gpsimd", "vect")
        p.wait("gpsimd", "scal")
        for i in range(NT):
            p.dma("gpsimd", x_out[b * T + 128 * i: b * T + 128 * (i + 1), :],
                  XT[:, 1024 * i:1024 * (i + 1)], "qO")

    def ln_stats(XT, scol):
        mu = stat[:, scol:scol + 16]
        rs = stat[:, scol + 16:scol + 32]
        nm = stat[:, scol + 32:scol + 48]
        sq = stat[:, scol + 48:scol + 64]
        for i in range(NT):
            xt = XT[:, 1024 * i:1024 * (i + 1)]
            p.op("scalar", "activation", bitb(M_SS)[:, 0:1024], xt, AF.Square,
                 accum_out=sq[:, i:i + 1],
                 inc="scal" if i == NT - 1 else None)
            p.op("vector", "reduce_sum", mu[:, i:i + 1], xt, AX.X,
                 inc="vect" if i == NT - 1 else None)
        p.wait("scalar", "scal")
        p.wait("vector", "vect")
        p.op("scalar", "mul", sq, sq, 1.0 / C, inc="scal")
        p.op("vector", "tensor_scalar_mul", mu, mu, 1.0 / C, inc="vect")
        p.wait("vector", "scal")
        p.wait("vector", "vect")
        p.op("vector", "tensor_mul", nm, mu, mu, inc="vect")
        p.wait("vector", "vect")
        p.op("vector", "tensor_sub", rs, sq, nm, inc="vect")
        p.wait("vector", "vect")
        p.op("vector", "tensor_scalar_add", rs, rs, EPS, inc="vect")
        p.wait("scalar", "vect")
        p.op("scalar", "activation", rs, rs, AF.Ln)
        p.op("scalar", "activation", rs, rs, AF.Exp, scale=-0.5, inc="scal")
        p.wait("vector", "scal")
        p.op("vector", "tensor_mul", nm, mu, rs, inc="vect")
        p.wait("vector", "vect")
        p.op("vector", "tensor_scalar_mul", nm, nm, -1.0, inc="vect")
        p.wait("scalar", "vect")

    def ln_apply(XT, i, scol, out_ap, inc=None):
        rs = stat[:, scol + 16 + i:scol + 17 + i]
        nm = stat[:, scol + 32 + i:scol + 33 + i]
        p.op("scalar", "activation", out_ap, XT[:, 1024 * i:1024 * (i + 1)],
             AF.Identity, bias=nm, scale=rs, inc=inc)

    tguard = Guard(p, 1)

    def transpose_tile_to_cm(src_bf_tile, dst_cm_bf, i, T_cols, nch=NC_):
        tguard.acquire(0, "tensor")
        for cc in range(nch):
            p.op("tensor", "transpose", PT[:, 128 * cc:128 * (cc + 1)],
                 src_bf_tile[:, 128 * cc:128 * (cc + 1)], idb[:],
                 inc="tens" if cc == nch - 1 else None)
        p.wait("vector", "tens")
        dst = dst_cm_bf.rearrange("p (cc t) -> p cc t", t=T_cols)[
            :, 0:nch, 128 * i:128 * (i + 1)]
        p.op("vector", "tensor_copy", dst,
             PT[:, 0:128 * nch].rearrange("p (cc r) -> p cc r", r=128), inc="vect")
        tguard.release(0, "vect")

    def make_cm(XT, scol, wb_bcast=None):
        ln_stats(XT, scol)
        if DBG == "stats" and not dumped[0]:
            dumped[0] = True
            p.barrier()
            p.dma("gpsimd", x_out[0:128, 0:384], stat[:], "qO")
            p.wait("gpsimd", "qO")
            return
        ntiles = NT
        if DBG.startswith("cmn") and not dumped[0]:
            k = int(DBG[3:])
            for i in range(k):
                stg = bitb(M_SS)[:, 0:1024]
                p.wait("scalar", "tens")
                ln_apply(XT, i, scol, stg, inc="scal")
                p.wait("tensor", "scal")
                transpose_tile_to_cm(stg, CM, i, T)
            dump_bf(CM, 2048 * k if k <= 4 else NC_ * T)
            return
        if DBG == "stage" and not dumped[0]:
            dumped[0] = True
            stg = bitb(M_SS)[:, 0:1024]
            p.wait("scalar", "tens")
            ln_apply(XT, 0, scol, stg, inc="scal")
            p.barrier()
            p.op("scalar", "copy", SCR[:, 0:1024], stg, inc="scal")
            p.wait("gpsimd", "scal")
            p.dma("gpsimd", x_out[0:128, :], SCR[:, 0:1024], "qO")
            p.wait("gpsimd", "qO")
            return
        for i in range(NT):
            stg = bitb(M_SS)[:, 0:1024]
            p.wait("scalar", "tens")
            if wb_bcast is None:
                ln_apply(XT, i, scol, stg, inc="scal")
                p.wait("tensor", "scal")
                transpose_tile_to_cm(stg, CM, i, T)
            else:
                wbc, bbc = wb_bcast
                f = SCR[:, 0:1024]
                ln_apply(XT, i, scol, f, inc="scal")
                p.wait("vector", "scal")
                p.op("vector", "tensor_mul", f, f, wbc)
                p.op("vector", "tensor_add", stg, f, bbc, inc="vect")
                p.wait("tensor", "vect")
                transpose_tile_to_cm(stg, CM, i, T)

    def conv_cm():
        for d in range(NC_):
            h = CM[:, T * d:T * (d + 1)]
            w0 = smallv[:, 160 + 3 * d:161 + 3 * d]
            w1 = smallv[:, 161 + 3 * d:162 + 3 * d]
            w2 = smallv[:, 162 + 3 * d:163 + 3 * d]
            u = SCR[:, 0:1024]
            for hf in (1, 0):
                b0 = 1024 * hf
                p.wait("vector", "scal")
                p.op("vector", "tensor_scalar_mul", u, h[:, b0:b0 + 1024], w2)
                if hf == 0:
                    p.op("vector", "scalar_tensor_tensor", u[:, 1:1024],
                         h[:, 0:1023], w1, u[:, 1:1024], ALU.mult, ALU.add)
                    p.op("vector", "tensor_copy", BNC[:, 0:2], u[:, 0:2])
                    p.op("vector", "scalar_tensor_tensor", BNC[:, 2:1024],
                         h[:, 0:1022], w0, u[:, 2:1024], ALU.mult, ALU.add,
                         inc="vect")
                else:
                    p.op("vector", "scalar_tensor_tensor", u,
                         h[:, b0 - 1:b0 + 1023], w1, u, ALU.mult, ALU.add)
                    p.op("vector", "scalar_tensor_tensor", BNC[:, 0:1024],
                         h[:, b0 - 2:b0 + 1022], w0, u, ALU.mult, ALU.add,
                         inc="vect")
                p.wait("scalar", "vect")
                p.op("scalar", "copy", h[:, b0:b0 + 1024], BNC[:, 0:1024],
                     inc="scal")
        p.wait("vector", "scal")
        p.wait("tensor", "scal")

    def stage_attn(XT, XF, inner):
        xf_bf = bitb(XF)
        QTd = xf_bf[:, 0:2048]
        KTd = xf_bf[:, 2048:4096]
        Vd = xf_bf[:, 4096:6272]
        EXPS = xf_bf[:, 8192:16384]
        Yb = xf_bf[:, 16384:16896]
        YTb = xf_bf[:, 16896:17408]
        if not inner:
            dma_bcast("gpsimd", M_BCA, row(bo), "qBA")
            p.wait("vector", "qBA")
            p.wait("vector", "scal")
            for i in range(NT):
                p.op("vector", "tensor_add", XT[:, 1024 * i:1024 * (i + 1)],
                     XT[:, 1024 * i:1024 * (i + 1)], M_BCA, inc="vect")
            dma_bcast("gpsimd", M_BCB, row(bv), "qBB")
            p.wait("vector", "qBB")
        for d in range(NC_):
            if inner:
                qsrc = lambda kc: attn_w_bf[128 * kc:128 * (kc + 1), 128 * d:128 * (d + 1)]
                ksrc = lambda kc: attn_w_bf[128 * kc:128 * (kc + 1), C + 128 * d:C + 128 * (d + 1)]
                vsrc = lambda kc: attn_w_bf[128 * kc:128 * (kc + 1), 2 * C + 128 * d:2 * C + 128 * (d + 1)]
            else:
                qsrc = lambda kc: wq_bf[128 * kc:128 * (kc + 1), 128 * d:128 * (d + 1)]
                ksrc = lambda kc: wk_bf[128 * kc:128 * (kc + 1), 128 * d:128 * (d + 1)]
                vsrc = lambda kc: wv_bf[128 * kc:128 * (kc + 1), 128 * d:128 * (d + 1)]
            for which, src, DST, bcol in (("q", qsrc, QTd, 0), ("k", ksrc, KTd, 8)):
                sset = 0 if which == "q" else 1
                sem = load_units((2 * sset, 2 * sset + 1),
                                 [(qk_slot(sset, kc), src(kc)) for kc in range(NC_)])
                p.wait("tensor", sem)
                for nsl in range(4):
                    pu = 2 * sset + (nsl % 2)
                    pb = phalf(sset, nsl % 2)
                    pguard.acquire(pu, "tensor")
                    for kc in range(NC_):
                        p.op("tensor", "matmul", pb,
                             qk_slot(sset, kc),
                             CM[:, T * kc + 512 * nsl: T * kc + 512 * (nsl + 1)],
                             start=(kc == 0), stop=(kc == NC_ - 1),
                             inc="tens" if kc == NC_ - 1 else None)
                    release_units((2 * sset, 2 * sset + 1), "tens")
                    p.wait("scalar", "tens")
                    if inner:
                        p.op("scalar", "copy",
                             DST[:, 512 * nsl:512 * (nsl + 1)], pb, inc="scal")
                    else:
                        p.op("scalar", "activation",
                             DST[:, 512 * nsl:512 * (nsl + 1)], pb, AF.Identity,
                             bias=smallv[:, bcol + d:bcol + d + 1], scale=1.0,
                             inc="scal")
                    pguard.release(pu, "scal")
                if inner:
                    p.wait("tensor", "scal")
                    for nsl in range(4):
                        pu = 4
                        pb = P[2][:, 0:512]
                        pguard.acquire(pu, "tensor")
                        p.op("tensor", "matmul", pb, rmt[:],
                             DST[:, 512 * nsl:512 * (nsl + 1)],
                             start=True, stop=True, inc="tens")
                        p.wait("vector", "tens")
                        s1 = SCR[:, 0:512]
                        s2 = SCR[:, 512:1024]
                        p.op("vector", "tensor_mul", s1, pb,
                             sin_dm[:, 512 * nsl:512 * (nsl + 1)])
                        pguard.release(pu, "vect")
                        p.op("vector", "tensor_mul", s2,
                             DST[:, 512 * nsl:512 * (nsl + 1)],
                             cos_dm[:, 512 * nsl:512 * (nsl + 1)])
                        p.op("vector", "tensor_add",
                             DST[:, 512 * nsl:512 * (nsl + 1)], s1, s2, inc="vect")
                        p.wait("tensor", "vect")
            if d == 0 and DBG in ("qt", "kt"):
                dump_bf(QTd if DBG == "qt" else KTd, 2048)
                return
            sem = load_units((4, 5),
                             [(v_slot(kc), vsrc(kc)) for kc in range(NC_)])
            p.wait("tensor", sem)
            p.wait("vector", "tens")
            p.op("vector", "memset", Vd[:], 0.0, inc="vect")
            p.wait("tensor", "vect")
            for tt in range(NT):
                pu = 6 + (tt % 2)
                pb = phalf(3, tt % 2)[:, 0:128]
                pguard.acquire(pu, "tensor")
                for kc in range(NC_):
                    p.op("tensor", "matmul", pb,
                         CM[:, T * kc + 128 * tt: T * kc + 128 * (tt + 1)],
                         v_slot(kc),
                         start=(kc == 0), stop=(kc == NC_ - 1),
                         inc="tens" if kc == NC_ - 1 else None)
                p.wait("vector", "tens")
                vt = Vd[:, 136 * tt:136 * (tt + 1)]
                dst3 = vt.rearrange("p (h s) -> p h s", s=VS)[:, :, 0:64]
                src3 = pb.rearrange("p (h s) -> p h s", s=64)
                if inner:
                    p.op("vector", "tensor_copy", dst3, src3, inc="vect")
                else:
                    bc3 = M_BCB[:, 128 * d:128 * (d + 1)].rearrange(
                        "p (h s) -> p h s", s=64)
                    p.op("vector", "tensor_add", dst3, src3, bc3, inc="vect")
                pguard.release(pu, "vect")
                p.op("vector", "memset",
                     vt.rearrange("p (h s) -> p h s", s=VS)[:, :, 64:65], 1.0,
                     inc="vect")
            release_units((4, 5), "tens")
            if d == 0 and DBG == "v":
                dump_bf(Vd, 2048)
                return
            for qb in range(T // QB):
                kts = range(min(NT, 2 * qb + 2)) if inner else range(NT)
                for hh in range(2):
                    dbase = 64 * hh
                    for kt in kts:
                        pu = 2 * (kt % 2) + ((kt // 2) % 2)
                        pb = phalf(kt % 2, (kt // 2) % 2)[:, 0:QB]
                        pguard.acquire(pu, "tensor")
                        p.op("tensor", "matmul", pb,
                             KTd[dbase:dbase + HD, 128 * kt:128 * (kt + 1)],
                             QTd[dbase:dbase + HD, QB * qb:QB * (qb + 1)],
                             start=True, stop=True, inc="tens")
                        dst = EXPS[:, (hh * NT + kt) * QB:(hh * NT + kt + 1) * QB]
                        p.wait("scalar", "tens")
                        p.op("scalar", "activation", dst, pb, AF.Exp,
                             scale=0.125, inc="scal")
                        pguard.release(pu, "scal")
                        if inner and kt >= 2 * qb:
                            rel = 128 * kt - QB * qb
                            mi = 0 if rel == 0 else 1
                            p.wait("vector", "scal")
                            p.op("vector", "tensor_mul", dst, dst,
                                 cmask[:, QB * mi:QB * (mi + 1)], inc="vect")
                p.wait("tensor", "scal")
                p.wait("tensor", "vect")
                for qt2 in range(QB // 128):
                    pvkts = range(min(NT, 2 * qb + qt2 + 1)) if inner else range(NT)
                    lkt = max(pvkts)
                    for hh in range(2):
                        pu = 4
                        pb = P[2][:, 128 * hh:128 * hh + VS]
                        pguard.acquire(pu, "tensor")
                        for kt in pvkts:
                            lhs = EXPS[:, (hh * NT + kt) * QB + 128 * qt2:
                                       (hh * NT + kt) * QB + 128 * (qt2 + 1)]
                            rhs = Vd[:, 136 * kt + VS * hh:136 * kt + VS * (hh + 1)]
                            p.op("tensor", "matmul", pb, lhs, rhs,
                                 start=(kt == 0), stop=(kt == lkt),
                                 inc="tens" if kt == lkt else None)
                        p.wait("vector", "tens")
                        p.op("vector", "reciprocal", stat[:, 380:381], pb[:, 64:65],
                             inc="vect")
                        p.wait("vector", "vect")
                        p.op("vector", "tensor_scalar_mul",
                             Yb[:, 128 * qt2 + 64 * hh:128 * qt2 + 64 * (hh + 1)],
                             pb[:, 0:64], stat[:, 380:381], inc="vect")
                        pguard.release(pu, "vect")
                if d == 0 and qb == 0 and DBG == "exps":
                    dump_bf(EXPS, 8192)
                    return
                if d == 0 and qb == 0 and DBG == "y":
                    dump_bf(Yb, 256)
                    return
                p.wait("tensor", "vect")
                tguard.acquire(0, "tensor")
                for qt2 in range(QB // 128):
                    p.op("tensor", "transpose", PT[:, 128 * qt2:128 * (qt2 + 1)],
                         Yb[:, 128 * qt2:128 * (qt2 + 1)], idb[:],
                         inc="tens" if qt2 == 1 else None)
                p.wait("vector", "tens")
                p.op("vector", "tensor_copy", YTb[:, 0:256], PT[:, 0:256],
                     inc="vect")
                tguard.release(0, "vect")
                p.wait("tensor", "vect")
                wsrc = (proj_w_bf if inner else wo_bf)
                for nsl in range(2):
                    j = (d * 2 + nsl) % 2
                    sem = load_units((6 + j,),
                                     [(op_slot(j),
                                       wsrc[128 * d:128 * (d + 1),
                                            512 * nsl:512 * (nsl + 1)])])
                    p.wait("tensor", sem)
                    for qt2 in range(QB // 128):
                        tt = (QB * qb) // 128 + qt2
                        pu = 6 + (qt2 % 2)
                        pb = phalf(3, qt2 % 2)
                        pguard.acquire(pu, "tensor")
                        p.op("tensor", "matmul", pb,
                             YTb[:, 128 * qt2:128 * (qt2 + 1)], op_slot(j),
                             start=True, stop=True, inc="tens")
                        p.wait("vector", "tens")
                        xsl = XT[:, 1024 * tt + 512 * nsl:1024 * tt + 512 * (nsl + 1)]
                        p.op("vector", "tensor_add", xsl, xsl, pb, inc="vect")
                        pguard.release(pu, "vect")
                    release_units((6 + j,), "tens")
        p.barrier()


    def park(XT):
        for i in range(NT):
            p.dma("gpsimd", park_x[128 * i:128 * (i + 1), :],
                  XT[:, 1024 * i:1024 * (i + 1)], "qP")

    unp_marks = []

    def unpark_add(XF):
        for i in range(NT):
            j = i % 2
            stg = BIG[:, 1024 * j:1024 * (j + 1)]
            if i >= 2:
                p.wait("sync", "vect", unp_marks[i - 2])
            p.dma("sync", stg, park_x[128 * i:128 * (i + 1), :], f"up{j}")
            p.wait("vector", f"up{j}")
            p.op("vector", "tensor_add", XF[:, 1024 * i:1024 * (i + 1)],
                 XF[:, 1024 * i:1024 * (i + 1)], stg, inc="vect")
            unp_marks.append(p.counts["vect"])
        p.wait("scalar", "vect")
        p.wait("tensor", "vect")
        p.wait("gpsimd", "vect")

    def stage_ffn(XT, XF, scol_base):
        ln_stats(XT, scol_base)
        p.wait("gpsimd", "qO")
        for i in range(NT):
            ln_apply(XT, i, scol_base, XF[:, 1024 * i:1024 * (i + 1)],
                     inc="scal")
        p.wait("gpsimd", "scal")
        p.wait("gpsimd", "qP")
        p.wait("gpsimd", "up0")
        p.wait("gpsimd", "up1")
        park(XT)
        dma_bcast("gpsimd", M_BCA, row(ln1_w), "qBA")
        dma_bcast("gpsimd", M_BCB, row(ln1_b), "qBB")
        p.wait("vector", "qBA")
        p.wait("vector", "qBB")
        p.wait("vector", "scal")
        make_cm(XF, scol_base + 64, wb_bcast=(M_BCA, M_BCB))
        conv_cm()
        stage_attn(XF, XT, inner=True)
        dma_bcast("gpsimd", M_BCA, row(ln2_w), "qBA")
        dma_bcast("gpsimd", M_BCB, row(ln2_b), "qBB")
        p.wait("vector", "qBA")
        p.wait("vector", "qBB")
        moe(XF, XT, scol_base + 128)
        unp_marks.clear()
        unpark_add(XF)
        p.barrier()

    def moe(XF, XT, scol):
        xt_bf = bitb(XT)
        ln_stats(XF, scol)
        for i in range(NT):
            f = SCR[:, 0:1024]
            p.wait("scalar", "tens")
            p.wait("scalar", "vect")
            ln_apply(XF, i, scol, f, inc="scal")
            p.wait("vector", "scal")
            p.op("vector", "tensor_mul", f, f, M_BCA)
            p.op("vector", "tensor_add", f, f, M_BCB, inc="vect")
            p.wait("tensor", "vect")
            tguard.acquire(0, "tensor")
            for cc in range(NC_):
                p.op("tensor", "transpose", P[3][:, 128 * cc:128 * (cc + 1)],
                     f[:, 128 * cc:128 * (cc + 1)], idf[:],
                     inc="tens" if cc == NC_ - 1 else None)
            p.wait("vector", "tens")
            dst = XT.rearrange("p (cc t) -> p cc t", t=T)[:, :, 128 * i:128 * (i + 1)]
            p.op("vector", "tensor_copy", dst,
                 P[3].rearrange("p (cc r) -> p cc r", r=128), inc="vect")
            tguard.release(0, "vect")
        for cc in range(NC_):
            p.dma("sync", rw[:, 4 * cc:4 * (cc + 1)],
                  router_w[128 * cc:128 * (cc + 1), :], "qR")
        p.wait("tensor", "qR")
        p.wait("tensor", "vect")
        pguard.acquire(4, "tensor")
        for i in range(NT):
            for cc in range(NC_):
                p.op("tensor", "matmul", P[2][:, 4 * i:4 * (i + 1)],
                     XT.rearrange("p (cc t) -> p cc t", t=T)[:, cc, 128 * i:128 * (i + 1)],
                     rw[:, 4 * cc:4 * (cc + 1)],
                     start=(cc == 0), stop=(cc == NC_ - 1),
                     inc="tens" if (cc == NC_ - 1 and i == NT - 1) else None)
        p.wait("vector", "tens")
        p.op("vector", "tensor_copy", logit[:], P[2][:, 0:64], inc="vect")
        pguard.release(4, "vect")
        L = logit.rearrange("p (t e) -> p t e", e=4)
        Me = emask.rearrange("p (t e) -> p t e", e=4)
        mx = stat[:, 368:384]
        run = stat[:, 352:368]

        def vop(*a, **k):
            p.op("vector", *a, inc="vect", **k)
            p.wait("vector", "vect")

        p.wait("vector", "vect")
        vop("tensor_max", mx, L[:, :, 0], L[:, :, 1])
        vop("tensor_max", mx, mx, L[:, :, 2])
        vop("tensor_max", mx, mx, L[:, :, 3])
        vop("memset", run, 1.0)
        for e in range(NEXP):
            ge = stat[:, 336:352]
            vop("tensor_tensor", ge, L[:, :, e], mx, ALU.is_ge)
            vop("tensor_mul", Me[:, :, e], ge, run)
            if e < NEXP - 1:
                vop("tensor_scalar", ge, ge, -1.0, 1.0, ALU.mult, ALU.add)
                vop("tensor_mul", run, run, ge)
        p.wait("vector", "tens")
        for cc in range(NC_):
            p.op("vector", "tensor_copy", CM[:, 2048 * cc:2048 * (cc + 1)],
                 XT[:, 2048 * cc:2048 * (cc + 1)],
                 inc="vect" if cc == NC_ - 1 else None)
        p.wait("tensor", "vect")
        p.wait("scalar", "vect")
        glu = xt_bf
        for e in range(NEXP):
            for tg in range(2):
                tok0 = 1024 * tg
                for m in range(32):
                    for which2, off in ((0, 0), (1, DFF)):
                        sem = load_units(
                            (2 * which2, 2 * which2 + 1),
                            [(qk_slot(which2, kc),
                              exp_wv_bf[C * e + 128 * kc:C * e + 128 * (kc + 1),
                                        off + 128 * m:off + 128 * (m + 1)])
                             for kc in range(NC_)])
                        p.wait("tensor", sem)
                    for ns in range(2):
                        pg = phalf(0, ns)
                        pv_ = phalf(1, ns)
                        pguard.acquire(0 + ns, "tensor")
                        pguard.acquire(2 + ns, "tensor")
                        for kc in range(NC_):
                            p.op("tensor", "matmul", pg, qk_slot(0, kc),
                                 CM[:, T * kc + tok0 + 512 * ns:
                                    T * kc + tok0 + 512 * (ns + 1)],
                                 start=(kc == 0), stop=(kc == NC_ - 1))
                        for kc in range(NC_):
                            p.op("tensor", "matmul", pv_, qk_slot(1, kc),
                                 CM[:, T * kc + tok0 + 512 * ns:
                                    T * kc + tok0 + 512 * (ns + 1)],
                                 start=(kc == 0), stop=(kc == NC_ - 1),
                                 inc="tens" if kc == NC_ - 1 else None)
                        p.wait("scalar", "tens")
                        sg = SCR[:, 0:512]
                        p.op("scalar", "activation", sg, pg, AF.Silu, inc="scal")
                        p.wait("vector", "scal")
                        p.op("vector", "tensor_mul",
                             glu[:, 1024 * m + 512 * ns:1024 * m + 512 * (ns + 1)],
                             sg, pv_, inc="vect")
                        pguard.release(0 + ns, "vect")
                        pguard.release(2 + ns, "vect")
                    release_units((0, 1, 2, 3), "tens")
                p.wait("tensor", "vect")
                for tt2 in range(8):
                    tt = 8 * tg + tt2
                    for nsl in range(2):
                        pu = 6 + (nsl % 2)
                        pb = phalf(3, nsl % 2)
                        pguard.acquire(pu, "tensor")
                        for octv in range(4):
                            sem = load_units(
                                tuple(range(8)),
                                [(punit(j),
                                  exp_proj_bf[DFF * e + 128 * (8 * octv + j):
                                              DFF * e + 128 * (8 * octv + j + 1),
                                              512 * nsl:512 * (nsl + 1)])
                                 for j in range(NC_)])
                            p.wait("tensor", sem)
                            for j in range(NC_):
                                kc = 8 * octv + j
                                p.op("tensor", "matmul", pb,
                                     glu[:, 1024 * kc + 128 * tt2:
                                         1024 * kc + 128 * (tt2 + 1)],
                                     punit(j),
                                     start=(kc == 0), stop=(kc == 31),
                                     inc="tens" if (j == NC_ - 1) else None)
                            release_units(tuple(range(8)), "tens")
                        p.wait("vector", "tens")
                        p.op("vector", "scalar_tensor_tensor",
                             XF[:, 1024 * tt + 512 * nsl:1024 * tt + 512 * (nsl + 1)],
                             pb, emask[:, 4 * tt + e:4 * tt + e + 1],
                             XF[:, 1024 * tt + 512 * nsl:1024 * tt + 512 * (nsl + 1)],
                             ALU.mult, ALU.add, inc="vect")
                        pguard.release(pu, "vect")

    def stage_ssm(XT, XF, scol):
        ln_stats(XT, scol)
        dma_bcast("gpsimd", M_BCA, row(D_par), "qBA")
        p.wait("vector", "qBA")
        for i in range(NT):
            f = SCR[:, 0:1024]
            p.wait("scalar", "tens")
            p.wait("scalar", "vect")
            ln_apply(XT, i, scol, f, inc="scal")
            p.wait("tensor", "scal")
            tguard.acquire(0, "tensor")
            for cc in range(NC_):
                p.op("tensor", "transpose", P[3][:, 128 * cc:128 * (cc + 1)],
                     f[:, 128 * cc:128 * (cc + 1)], idf[:],
                     inc="tens" if cc == NC_ - 1 else None)
            p.wait("vector", "scal")
            ud = BIG[:, 7168:8192]
            p.op("vector", "tensor_mul", ud, f, M_BCA)
            p.op("vector", "tensor_add", XT[:, 1024 * i:1024 * (i + 1)],
                 XT[:, 1024 * i:1024 * (i + 1)], ud)
            p.wait("vector", "tens")
            dst = XF.rearrange("p (cc t) -> p cc t", t=T)[:, :, 128 * i:128 * (i + 1)]
            p.op("vector", "tensor_copy", dst,
                 P[3].rearrange("p (cc r) -> p cc r", r=128), inc="vect")
            tguard.release(0, "vect")
        xdbl = POOL[0:96, :]
        p.wait("tensor", "vect")
        for sl4 in range(4):
            pguard.acquire(4, "tensor")
            for cc in range(NC_):
                p.op("tensor", "matmul", P[2][0:96, 0:512],
                     wxt[:, 96 * cc:96 * (cc + 1)],
                     XF[:, 2048 * cc + 512 * sl4:2048 * cc + 512 * (sl4 + 1)],
                     start=(cc == 0), stop=(cc == NC_ - 1),
                     inc="tens" if cc == NC_ - 1 else None)
            p.wait("vector", "tens")
            p.op("vector", "tensor_copy", xdbl[:, 512 * sl4:512 * (sl4 + 1)],
                 P[2][0:96, 0:512], inc="vect")
            pguard.release(4, "vect")
        p.wait("gpsimd", "vect")
        p.dma("gpsimd", bc_rows[:], xdbl[64:96, :], "qBC")
        bc_guard = [0]
        DT_ = BIG[:, 0:1024]
        DTU = BIG[:, 1024:2048]
        ABR = BIG[:, 2048:3072]
        HH_ = BIG[:, 3072:4096]
        TT_ = BIG[:, 2048:3072]
        BBC = BIG[:, 4096:5120]
        CBC = BIG[:, 5120:6144]
        YH = BIG[:, 6144:7168]
        for lh in range(2):
            l0 = 1024 * lh
            for d in range(NC_):
                for s2 in range(2):
                    pguard.acquire(4, "tensor")
                    p.op("tensor", "matmul", P[2][:, 0:512],
                         wdtt[:, 128 * d:128 * (d + 1)],
                         xdbl[0:64, l0 + 512 * s2:l0 + 512 * (s2 + 1)],
                         start=True, stop=True, inc="tens")
                    bdt = smallv[:, 16 + d:17 + d]
                    p.wait("scalar", "tens")
                    sl_ = slice(512 * s2, 512 * (s2 + 1))
                    p.op("scalar", "activation", DT_[:, sl_], P[2][:, 0:512],
                         AF.Relu, bias=bdt, scale=1.0)
                    p.op("scalar", "activation", SCR[:, 0:512], P[2][:, 0:512],
                         AF.Abs, bias=bdt, scale=1.0)
                    p.op("scalar", "activation", SCR[:, 0:512], SCR[:, 0:512],
                         AF.Exp, scale=-1.0)
                    p.op("scalar", "activation", SCR[:, 0:512], SCR[:, 0:512],
                         AF.Ln, bias=1.0, inc="scal")
                    p.wait("vector", "scal")
                    p.op("vector", "tensor_add", DT_[:, sl_], DT_[:, sl_],
                         SCR[:, 0:512], inc="vect")
                    pguard.release(4, "vect")
                p.op("vector", "tensor_mul", DTU, DT_,
                     XF[:, 2048 * d + l0:2048 * d + l0 + 1024], inc="vect")
                for n in range(16):
                    p.wait("sync", "qBC")
                    if True:
                        p.wait("sync", "vect", bc_guard[0])
                        p.dma("sync", BBC,
                              bc_rows[n:n + 1, l0:l0 + 1024].partition_broadcast(128),
                              "qBB2")
                        p.dma("sync", CBC,
                              bc_rows[16 + n:17 + n, l0:l0 + 1024].partition_broadcast(128),
                              "qBB2")
                    A_col = smallv[:, 24 + 16 * d + n:25 + 16 * d + n]
                    p.wait("scalar", "vect")
                    p.op("scalar", "activation", ABR, DT_, AF.Exp, scale=A_col,
                         inc="scal")
                    p.wait("vector", "qBB2")
                    p.wait("vector", "scal")
                    p.op("vector", "tensor_mul", SCR[:, 0:1024], DTU, BBC)
                    p.op("vector", "tensor_tensor_scan", HH_, ABR, SCR[:, 0:1024],
                         0.0 if lh == 0 else carry[:, 16 * d + n:16 * d + n + 1],
                         ALU.mult, ALU.add, inc="vect")
                    p.wait("vector", "vect")
                    p.op("vector", "tensor_copy", carry[:, 16 * d + n:16 * d + n + 1],
                         HH_[:, 1023:1024])
                    p.op("vector", "tensor_mul", TT_, HH_, CBC, inc="vect")
                    bc_guard[0] = p.counts["vect"]
                    if n == 0:
                        p.op("vector", "tensor_copy", YH, TT_, inc="vect")
                    else:
                        p.op("vector", "tensor_add", YH, YH, TT_, inc="vect")
                p.wait("tensor", "vect")
                tguard.acquire(0, "tensor")
                for j in range(NC_):
                    p.op("tensor", "transpose", P[3][:, 128 * j:128 * (j + 1)],
                         YH[:, 128 * j:128 * (j + 1)], idf[:],
                         inc="tens" if j == NC_ - 1 else None)
                p.wait("vector", "tens")
                dstx = XT.rearrange("p (tt t) -> p tt t", t=1024)[
                    :, 8 * lh:8 * lh + 8, 128 * d:128 * (d + 1)]
                p.op("vector", "tensor_add", dstx, dstx,
                     P[3].rearrange("p (j r) -> p j r", r=128), inc="vect")
                tguard.release(0, "vect")
        p.barrier()
    with es:
        load_consts()
        prologue_weights()
        bufs = [XA, XB]
        for b in range(BPC):
            load_x(b, bufs[0])
            for e in ("vector", "scalar", "tensor", "gpsimd"):
                p.wait(e, "qX")
            if DBG == "xa" and b == 0:
                dumped[0] = True
                p.barrier()
                for i in range(NT):
                    p.dma("gpsimd", x_out[128 * i:128 * (i + 1), :],
                          bufs[0][:, 1024 * i:1024 * (i + 1)], "qO")
                p.wait("gpsimd", "qO")
            if DBG in ("wq", "wq16") and b == 0:
                p.barrier()
                for r in range(NC_):
                    p.dma("sync", big_bf[:, 1024 * r:1024 * (r + 1)],
                          wq_bf[128 * r:128 * (r + 1), :], "qX")
                if DBG == "wq16":
                    for r in range(NC_):
                        p.dma("sync", big_bf[:, 8192 + 1024 * r:8192 + 1024 * (r + 1)],
                              wk_bf[128 * r:128 * (r + 1), :], "qX")
                p.wait("scalar", "qX")
                dump_bf(big_bf[:, 0:8192] if DBG == "wq" else big_bf[:, 0:16384],
                        8192 if DBG == "wq" else 16384)
            xcur, xfree = bufs[0], bufs[1]
            if STAGES >= 1:
                make_cm(xcur, 0)
                if DBG == "cm" and b == 0:
                    dump_bf(CM, NC_ * T)
                if not dumped[0]:
                    stage_attn(xcur, xfree, inner=False)
            if STAGES >= 2 and not dumped[0]:
                stage_ffn(xcur, xfree, 64)
                xcur, xfree = xfree, xcur
            if STAGES >= 3 and not dumped[0]:
                stage_ssm(xcur, xfree, 0)
            if STAGES >= 4 and not dumped[0]:
                stage_ffn(xcur, xfree, 64)
                xcur, xfree = xfree, xcur
            if not dumped[0]:
                store_x(b, xcur)
            if dumped[0]:
                break
        p.wait("gpsimd", "qO")
        p.barrier()

        with nc.Block() as block:
            for ename in ENGINES:
                def make(en):
                    def _(eng):
                        for fn in p.streams[en]:
                            fn(eng)
                    return _
                getattr(block, ename)(make(ename))
    return nc


def _constants():
    ident = np.eye(128, dtype=np.float32)
    ident_b = np.eye(128).astype(ml_dtypes.bfloat16)
    inv_freq = 1.0 / (10000.0 ** (np.arange(0, HD, 2, dtype=np.float64) / HD))
    t = np.arange(T, dtype=np.float64)
    freqs = np.einsum("i,j->ij", t, inv_freq)
    emb = np.concatenate([freqs, freqs], axis=-1)
    cos = np.cos(emb)
    sin = np.sin(emb)
    cos_dm = np.concatenate([cos.T, cos.T], axis=0).astype(ml_dtypes.bfloat16)
    sin_dm = np.concatenate([sin.T, sin.T], axis=0).astype(ml_dtypes.bfloat16)
    R = np.zeros((128, 128), dtype=np.float64)
    for hh in range(2):
        o = 64 * hh
        for dd in range(32):
            R[o + dd, o + dd + 32] = -1.0
            R[o + dd + 32, o + dd] = 1.0
    rmat = R.T.astype(ml_dtypes.bfloat16)
    kk = np.arange(128)[:, None]
    qq = np.arange(QB)[None, :]
    masks = np.concatenate(
        [((rel + kk) <= qq).astype(np.float64) for rel in (0, 128)],
        axis=1).astype(ml_dtypes.bfloat16)
    return ident, ident_b, cos_dm, sin_dm, rmat, masks


def kernel(**inputs):
    g = lambda k: np.ascontiguousarray(np.asarray(inputs[k], dtype=np.float32))
    x = g("L0_attn_a_x")
    ident, ident_b, cos_dm, sin_dm, rmat, masks = _constants()
    base = dict(
        Wq=g("L0_attn_a_Wq"), bq=g("L0_attn_a_bq"), Wk=g("L0_attn_a_Wk"),
        bk=g("L0_attn_a_bk"), Wv=g("L0_attn_a_Wv"), bv=g("L0_attn_a_bv"),
        Wo=g("L0_attn_a_Wo"), bo=g("L0_attn_a_bo"),
        A_log=g("L0_attn_b_A_log"), W_x=g("L0_attn_b_W_x"),
        W_dt=g("L0_attn_b_W_dt"), b_dt=g("L0_attn_b_b_dt"),
        D_par=g("L0_attn_b_D_param"),
        ln1_w=g("L0_ffn_ln1_w"), ln1_b=g("L0_ffn_ln1_b"),
        ln2_w=g("L0_ffn_ln2_w"), ln2_b=g("L0_ffn_ln2_b"),
        conv_w=g("L0_ffn_conv_w").reshape(C, 3),
        attn_w=g("L0_ffn_attn_w"), proj_w=g("L0_ffn_proj_w"),
        router_w=g("L0_ffn_router_w"),
        exp_wv=g("L0_ffn_expert_wv").reshape(NEXP * C, 2 * DFF),
        exp_proj=g("L0_ffn_expert_proj").reshape(NEXP * DFF, C),
        ident_f=ident, ident_b=ident_b, rmat=rmat,
        cos_dm_i=cos_dm, sin_dm_i=sin_dm, masks=masks,
    )
    in_maps = []
    for c in range(NCORES):
        m = dict(base)
        m["x_in"] = np.ascontiguousarray(
            x[BPC * c:BPC * (c + 1)].reshape(BPC * T, C))
        in_maps.append(m)
    nc = build_program()
    res = run_bass_kernel_spmd(nc, in_maps, core_ids=list(range(NCORES)))
    out = np.concatenate(
        [r["x_out"].reshape(BPC, T, C) for r in res.results], axis=0)
    return out.astype(np.float32)




# version 03
# speedup vs baseline: 2.2877x; 2.2877x over previous
import numpy as np
import ml_dtypes
import concourse.bass as bass
import concourse.mybir as mybir
from concourse.bass_utils import run_bass_kernel_spmd

F32 = mybir.dt.float32
BF16 = mybir.dt.bfloat16
AF = mybir.ActivationFunctionType
ALU = mybir.AluOpType
AX = mybir.AxisListType

B_FULL, T, C = 16, 2048, 1024
NCORES = 8
BPC = B_FULL // NCORES
NT = T // 128
NC_ = C // 128
H, HD = 16, 64
EPS = 1e-5
DFF = 4096
NEXP = 4
QB = 256
VS = 68

ENGINES = ("sync", "gpsimd", "tensor", "vector", "scalar")
STAGES = 4
DBG = ""


class Prog:
    def __init__(self, nc):
        self.nc = nc
        self.streams = {e: [] for e in ENGINES}
        self.sems = {}
        self.counts = {}
        self.waited = {e: {} for e in ENGINES}

    def sem(self, name):
        s = self.nc.semaphore(name).__enter__()
        self.sems[name] = s
        self.counts[name] = 0

    def emit(self, eng, fn):
        self.streams[eng].append(fn)

    def dma(self, eng, out, in_, sem_name, **kw):
        if sem_name not in self.sems:
            self.sem(sem_name)
        s = self.sems[sem_name]
        self.counts[sem_name] += 16
        self.emit(eng, lambda e: e.dma_start(out=out, in_=in_, **kw).then_inc(s, 16))

    def op(self, eng, method, *args, inc=None, **kw):
        if inc is None:
            self.emit(eng, lambda e: getattr(e, method)(*args, **kw))
        else:
            s = self.sems[inc]
            self.counts[inc] += 1
            self.emit(eng, lambda e: getattr(e, method)(*args, **kw).then_inc(s, 1))

    def wait(self, eng, sem_name, val=None):
        if sem_name not in self.counts:
            return
        if val is None:
            val = self.counts[sem_name]
        if val <= 0:
            return
        if val <= self.waited[eng].get(sem_name, 0):
            return
        self.waited[eng][sem_name] = val
        s = self.sems[sem_name]
        self.emit(eng, lambda e: e.wait_ge(s, val))

    def barrier(self):
        for e in ENGINES:
            s = self.sems["b_" + e]
            self.counts["b_" + e] += 1
            self.emit(e, lambda en, s=s: en.sem_inc(s, 1))
        for e in ENGINES:
            for o in ENGINES:
                self.wait(e, "b_" + o)


class Guard:
    def __init__(self, p, n):
        self.p = p
        self.g = [None] * n

    def acquire(self, i, eng):
        if self.g[i] is not None:
            self.p.wait(eng, self.g[i][0], self.g[i][1])

    def release(self, i, sem_name):
        self.g[i] = (sem_name, self.p.counts[sem_name])


def build_program(detect_races=True):
    nc = bass.Bass("TRN2", target_bir_lowering=False,
                   detect_race_conditions=detect_races)
    p = Prog(nc)
    for name in ("qS", "qG", "tens", "vect", "scal", "pool"):
        p.sem(name)
    for e in ENGINES:
        p.sem("b_" + e)

    dt_f = lambda n, sh: nc.dram_tensor(n, list(sh), F32, kind="ExternalInput").ap()
    dt_b = lambda n, sh: nc.dram_tensor(n, list(sh), BF16, kind="ExternalInput").ap()

    x_in = dt_f("x_in", (BPC * T, C))
    Wq = dt_f("Wq", (C, C)); bq = dt_f("bq", (C,))
    Wk = dt_f("Wk", (C, C)); bk = dt_f("bk", (C,))
    Wv = dt_f("Wv", (C, C)); bv = dt_f("bv", (C,))
    Wo = dt_f("Wo", (C, C)); bo = dt_f("bo", (C,))
    A_log = dt_f("A_log", (C, 16))
    W_x = dt_f("W_x", (96, C))
    W_dt = dt_f("W_dt", (C, 64))
    b_dt = dt_f("b_dt", (C,))
    D_par = dt_f("D_par", (C,))
    ln1_w = dt_f("ln1_w", (C,)); ln1_b = dt_f("ln1_b", (C,))
    ln2_w = dt_f("ln2_w", (C,)); ln2_b = dt_f("ln2_b", (C,))
    conv_w = dt_f("conv_w", (C, 3))
    attn_w = dt_f("attn_w", (C, 3 * C))
    proj_w = dt_f("proj_w", (C, C))
    router_w = dt_f("router_w", (C, NEXP))
    exp_wv = dt_f("exp_wv", (NEXP * C, 2 * DFF))
    exp_proj = dt_f("exp_proj", (NEXP * DFF, C))
    ident_f = dt_f("ident_f", (128, 128))
    ident_b = dt_b("ident_b", (128, 128))
    rmat = dt_b("rmat", (128, 128))
    cos_dm_in = dt_b("cos_dm_i", (128, T))
    sin_dm_in = dt_b("sin_dm_i", (128, T))
    masks_in = dt_b("masks", (128, 2 * QB))

    x_out = nc.dram_tensor("x_out", [BPC * T, C], F32, kind="ExternalOutput").ap()

    dram_i = lambda n, sh, dt: nc.dram_tensor(n, list(sh), dt, kind="Internal").ap()
    wq_bf = dram_i("wq_bf", (C, C), BF16)
    wk_bf = dram_i("wk_bf", (C, C), BF16)
    wv_bf = dram_i("wv_bf", (C, C), BF16)
    wo_bf = dram_i("wo_bf", (C, C), BF16)
    attn_w_bf = dram_i("attn_w_bf", (C, 3 * C), BF16)
    proj_w_bf = dram_i("proj_w_bf", (C, C), BF16)
    exp_wv_bf = dram_i("exp_wv_bf", (NEXP * C, 2 * DFF), BF16)
    exp_proj_bf = dram_i("exp_proj_bf", (NEXP * DFF, C), BF16)
    park_x = dram_i("park_x", (T, C), F32)
    bc_rows = dram_i("bc_rows", (32, T), F32)

    from contextlib import ExitStack
    es = ExitStack()
    sb = lambda n, sh, dt: es.enter_context(nc.sbuf_tensor(n, list(sh), dt))
    XA = sb("XA", (128, NT * 1024), F32)
    XB = sb("XB", (128, NT * 1024), F32)
    BIG = sb("BIG", (128, 9728), F32)
    POOL = sb("POOL", (128, 2048), F32)
    MISC = sb("MISC", (128, 3072), F32)
    idf = sb("idf", (128, 128), F32)
    idb = sb("idb", (128, 128), BF16)
    rmt = sb("rmt", (128, 128), BF16)
    cos_dm = sb("cos_dm", (128, T), BF16)
    sin_dm = sb("sin_dm", (128, T), BF16)
    stat = sb("stat", (128, 384), F32)
    smallv = sb("smallv", (128, 256), F32)
    wxt = sb("wxt", (128, 768), F32)
    wdtt = sb("wdtt", (64, 1024), F32)
    cmask = sb("cmask", (128, 2 * QB), BF16)
    carry = sb("carry", (128, 128), F32)
    logit = sb("logit", (128, 64), F32)
    emask = sb("emask", (128, 64), F32)
    rw = sb("rw", (128, 32), F32)
    P = [es.enter_context(nc.psum_tensor("P0", [128, 1024], F32)),
         es.enter_context(nc.psum_tensor("P1", [128, 1024], F32)),
         es.enter_context(nc.psum_tensor("P2", [128, 512], F32)),
         es.enter_context(nc.psum_tensor("P3", [128, 1024], F32))]
    PT = es.enter_context(nc.psum_tensor("PT", [128, 1024], BF16))

    M_BCA = MISC[:, 0:1024]
    M_BCB = MISC[:, 1024:2048]
    M_SS = MISC[:, 2048:3072]

    bitb = lambda ap: ap.bitcast(BF16)
    pool_bf = bitb(POOL)
    big_bf = bitb(BIG)

    CM = big_bf[:, 0:NC_ * T]
    SCR = BIG[:, 8192:9728]
    BNC = big_bf[:, 18432:19456]

    slots = Guard(p, 8)

    def punit(u):
        return pool_bf[:, 512 * u:512 * (u + 1)]

    def qk_slot(sset, kc):
        return pool_bf[:, 128 * (8 * sset + kc):128 * (8 * sset + kc) + 128]

    def v_slot(kc):
        return pool_bf[:, 2048 + 128 * kc:2048 + 128 * (kc + 1)]

    def op_slot(j):
        return pool_bf[:, 3072 + 512 * j:3072 + 512 * (j + 1)]

    def load_units(units, loads):
        for u in units:
            slots.acquire(u, "sync")
        sem = f"pu{units[0]}"
        for dst, srcap in loads:
            p.dma("sync", dst, srcap, sem)
        return sem

    def release_units(units, sem_name):
        for u in units:
            slots.release(u, sem_name)

    pguard = Guard(p, 8)

    def phalf(i, j):
        return P[i][:, 512 * j:512 * (j + 1)]

    dumped = [False]

    def dump_bf(ap_bf, ncols):
        if dumped[0]:
            return
        dumped[0] = True
        p.barrier()
        nblk = ncols // 1024
        for i in range(nblk):
            p.op("scalar", "copy", SCR[:, 0:1024], ap_bf[:, 1024 * i:1024 * (i + 1)],
                 inc="scal")
            p.wait("gpsimd", "scal")
            p.dma("gpsimd", x_out[128 * i:128 * (i + 1), :], SCR[:, 0:1024], "qO")
            p.wait("scalar", "qO")
        p.barrier()

    def load_consts():
        p.dma("sync", idf[:], ident_f, "qC")
        p.dma("sync", idb[:], ident_b, "qC")
        p.dma("sync", rmt[:], rmat, "qC")
        p.dma("sync", cos_dm[:], cos_dm_in, "qC")
        p.dma("sync", sin_dm[:], sin_dm_in, "qC")
        p.dma("sync", cmask[:], masks_in, "qC")
        for cc in range(NC_):
            sl = slice(128 * cc, 128 * (cc + 1))
            p.dma("sync", smallv[:, cc:cc + 1], bq[sl].rearrange("(p o) -> p o", o=1), "qC")
            p.dma("sync", smallv[:, 8 + cc:9 + cc], bk[sl].rearrange("(p o) -> p o", o=1), "qC")
            p.dma("sync", smallv[:, 16 + cc:17 + cc], b_dt[sl].rearrange("(p o) -> p o", o=1), "qC")
            p.dma("sync", smallv[:, 24 + 16 * cc:24 + 16 * (cc + 1)], A_log[sl, :], "qC")
            p.dma("sync", smallv[:, 160 + 3 * cc:163 + 3 * cc], conv_w[sl, :], "qC")
        p.wait("scalar", "qC")
        p.op("scalar", "activation", smallv[:, 24:152], smallv[:, 24:152], AF.Exp)
        p.op("scalar", "mul", smallv[:, 24:152], smallv[:, 24:152], -1.0, inc="scal")

    def prologue_weights():
        jobs = [(Wq, wq_bf, C, C), (Wk, wk_bf, C, C), (Wv, wv_bf, C, C),
                (Wo, wo_bf, C, C), (attn_w, attn_w_bf, C, 3 * C),
                (proj_w, proj_w_bf, C, C),
                (exp_wv, exp_wv_bf, NEXP * C, 2 * DFF),
                (exp_proj, exp_proj_bf, NEXP * DFF, C)]
        inb = Guard(p, 2)
        outb = Guard(p, 2)
        nb = 0
        for src, dst, rows, cols in jobs:
            for r in range(rows // 128):
                for c0 in range(0, cols, 1024):
                    w = min(1024, cols - c0)
                    i = nb % 2
                    buf = POOL[:, 1024 * i:1024 * i + w]
                    ob = big_bf[:, 16384 + 1024 * i:16384 + 1024 * i + w]
                    inb.acquire(i, "sync")
                    p.dma("sync", buf, src[128 * r:128 * (r + 1), c0:c0 + w], f"pi{i}")
                    outb.acquire(i, "vector")
                    p.wait("vector", f"pi{i}")
                    p.op("vector", "tensor_copy", ob, buf, inc="vect")
                    inb.release(i, "vect")
                    p.wait("gpsimd", "vect")
                    p.dma("gpsimd", dst[128 * r:128 * (r + 1), c0:c0 + w], ob, f"po{i}")
                    outb.release(i, f"po{i}")
                    nb += 1
        WXS = BIG[:, 0:1024]
        p.wait("vector", "po0")
        p.wait("vector", "po1")
        p.op("vector", "memset", WXS, 0.0, inc="vect")
        p.wait("sync", "vect")
        p.dma("sync", WXS[0:96, :], W_x, "qW")
        p.wait("tensor", "qW")
        for cc in range(NC_):
            p.op("tensor", "transpose", P[0][:, 0:128], WXS[:, 128 * cc:128 * (cc + 1)],
                 idf[:], inc="tens")
            p.wait("scalar", "tens")
            p.op("scalar", "copy", wxt[:, 96 * cc:96 * (cc + 1)], P[0][:, 0:96], inc="scal")
            p.wait("tensor", "scal")
        p.wait("sync", "scal")
        p.wait("sync", "qW")
        for cc in range(NC_):
            p.dma("sync", BIG[:, 1024 + 64 * cc:1024 + 64 * (cc + 1)],
                  W_dt[128 * cc:128 * (cc + 1), :], "qW")
        p.wait("tensor", "qW")
        for cc in range(NC_):
            p.op("tensor", "transpose", P[0][0:64, 128:256],
                 BIG[:, 1024 + 64 * cc:1024 + 64 * (cc + 1)], idf[:], inc="tens")
            p.wait("scalar", "tens")
            p.op("scalar", "copy", wdtt[:, 128 * cc:128 * (cc + 1)], P[0][0:64, 128:256],
                 inc="scal")
            p.wait("tensor", "scal")
        p.barrier()

    def dma_bcast(eng, dst_tile, src_row_ap, q):
        p.dma(eng, dst_tile, src_row_ap.partition_broadcast(128), q)

    def row(v):
        return v.rearrange("(o c) -> o c", o=1)

    def load_x(b, XT):
        p.wait("sync", "qO")
        for i in range(NT):
            p.dma("sync", XT[:, 1024 * i:1024 * (i + 1)],
                  x_in[b * T + 128 * i: b * T + 128 * (i + 1), :], "qX")

    def store_x(b, XT):
        p.wait("gpsimd", "vect")
        p.wait("gpsimd", "scal")
        for i in range(NT):
            p.dma("gpsimd", x_out[b * T + 128 * i: b * T + 128 * (i + 1), :],
                  XT[:, 1024 * i:1024 * (i + 1)], "qO")

    def ln_stats(XT, scol):
        mu = stat[:, scol:scol + 16]
        rs = stat[:, scol + 16:scol + 32]
        nm = stat[:, scol + 32:scol + 48]
        sq = stat[:, scol + 48:scol + 64]
        for i in range(NT):
            xt = XT[:, 1024 * i:1024 * (i + 1)]
            p.op("scalar", "activation", bitb(M_SS)[:, 0:1024], xt, AF.Square,
                 accum_out=sq[:, i:i + 1],
                 inc="scal" if i == NT - 1 else None)
            p.op("vector", "reduce_sum", mu[:, i:i + 1], xt, AX.X,
                 inc="vect" if i == NT - 1 else None)
        p.wait("scalar", "scal")
        p.wait("vector", "vect")
        p.op("scalar", "mul", sq, sq, 1.0 / C, inc="scal")
        p.op("vector", "tensor_scalar_mul", mu, mu, 1.0 / C, inc="vect")
        p.wait("vector", "scal")
        p.wait("vector", "vect")
        p.op("vector", "tensor_mul", nm, mu, mu, inc="vect")
        p.wait("vector", "vect")
        p.op("vector", "tensor_sub", rs, sq, nm, inc="vect")
        p.wait("vector", "vect")
        p.op("vector", "tensor_scalar_add", rs, rs, EPS, inc="vect")
        p.wait("scalar", "vect")
        p.op("scalar", "activation", rs, rs, AF.Ln)
        p.op("scalar", "activation", rs, rs, AF.Exp, scale=-0.5, inc="scal")
        p.wait("vector", "scal")
        p.op("vector", "tensor_mul", nm, mu, rs, inc="vect")
        p.wait("vector", "vect")
        p.op("vector", "tensor_scalar_mul", nm, nm, -1.0, inc="vect")
        p.wait("scalar", "vect")

    def ln_apply(XT, i, scol, out_ap, inc=None):
        rs = stat[:, scol + 16 + i:scol + 17 + i]
        nm = stat[:, scol + 32 + i:scol + 33 + i]
        p.op("scalar", "activation", out_ap, XT[:, 1024 * i:1024 * (i + 1)],
             AF.Identity, bias=nm, scale=rs, inc=inc)

    tguard = Guard(p, 1)

    def transpose_tile_to_cm(src_bf_tile, dst_cm_bf, i, T_cols, nch=NC_):
        tguard.acquire(0, "tensor")
        for cc in range(nch):
            p.op("tensor", "transpose", PT[:, 128 * cc:128 * (cc + 1)],
                 src_bf_tile[:, 128 * cc:128 * (cc + 1)], idb[:],
                 inc="tens" if cc == nch - 1 else None)
        p.wait("vector", "tens")
        dst = dst_cm_bf.rearrange("p (cc t) -> p cc t", t=T_cols)[
            :, 0:nch, 128 * i:128 * (i + 1)]
        p.op("vector", "tensor_copy", dst,
             PT[:, 0:128 * nch].rearrange("p (cc r) -> p cc r", r=128), inc="vect")
        tguard.release(0, "vect")

    def make_cm(XT, scol, wb_bcast=None):
        ln_stats(XT, scol)
        if DBG == "stats" and not dumped[0]:
            dumped[0] = True
            p.barrier()
            p.dma("gpsimd", x_out[0:128, 0:384], stat[:], "qO")
            p.wait("gpsimd", "qO")
            return
        ntiles = NT
        if DBG.startswith("cmn") and not dumped[0]:
            k = int(DBG[3:])
            for i in range(k):
                stg = bitb(M_SS)[:, 0:1024]
                p.wait("scalar", "tens")
                ln_apply(XT, i, scol, stg, inc="scal")
                p.wait("tensor", "scal")
                transpose_tile_to_cm(stg, CM, i, T)
            dump_bf(CM, 2048 * k if k <= 4 else NC_ * T)
            return
        if DBG == "stage" and not dumped[0]:
            dumped[0] = True
            stg = bitb(M_SS)[:, 0:1024]
            p.wait("scalar", "tens")
            ln_apply(XT, 0, scol, stg, inc="scal")
            p.barrier()
            p.op("scalar", "copy", SCR[:, 0:1024], stg, inc="scal")
            p.wait("gpsimd", "scal")
            p.dma("gpsimd", x_out[0:128, :], SCR[:, 0:1024], "qO")
            p.wait("gpsimd", "qO")
            return
        for i in range(NT):
            stg = bitb(M_SS)[:, 0:1024]
            p.wait("scalar", "tens")
            if wb_bcast is None:
                ln_apply(XT, i, scol, stg, inc="scal")
                p.wait("tensor", "scal")
                transpose_tile_to_cm(stg, CM, i, T)
            else:
                wbc, bbc = wb_bcast
                f = SCR[:, 0:1024]
                ln_apply(XT, i, scol, f, inc="scal")
                p.wait("vector", "scal")
                p.op("vector", "tensor_mul", f, f, wbc)
                p.op("vector", "tensor_add", stg, f, bbc, inc="vect")
                p.wait("tensor", "vect")
                transpose_tile_to_cm(stg, CM, i, T)

    def conv_cm():
        for d in range(NC_):
            h = CM[:, T * d:T * (d + 1)]
            w0 = smallv[:, 160 + 3 * d:161 + 3 * d]
            w1 = smallv[:, 161 + 3 * d:162 + 3 * d]
            w2 = smallv[:, 162 + 3 * d:163 + 3 * d]
            u = SCR[:, 0:1024]
            for hf in (1, 0):
                b0 = 1024 * hf
                p.wait("vector", "scal")
                p.op("vector", "tensor_scalar_mul", u, h[:, b0:b0 + 1024], w2)
                if hf == 0:
                    p.op("vector", "scalar_tensor_tensor", u[:, 1:1024],
                         h[:, 0:1023], w1, u[:, 1:1024], ALU.mult, ALU.add)
                    p.op("vector", "tensor_copy", BNC[:, 0:2], u[:, 0:2])
                    p.op("vector", "scalar_tensor_tensor", BNC[:, 2:1024],
                         h[:, 0:1022], w0, u[:, 2:1024], ALU.mult, ALU.add,
                         inc="vect")
                else:
                    p.op("vector", "scalar_tensor_tensor", u,
                         h[:, b0 - 1:b0 + 1023], w1, u, ALU.mult, ALU.add)
                    p.op("vector", "scalar_tensor_tensor", BNC[:, 0:1024],
                         h[:, b0 - 2:b0 + 1022], w0, u, ALU.mult, ALU.add,
                         inc="vect")
                p.wait("scalar", "vect")
                p.op("scalar", "copy", h[:, b0:b0 + 1024], BNC[:, 0:1024],
                     inc="scal")
        p.wait("vector", "scal")
        p.wait("tensor", "scal")

    def stage_attn(XT, XF, inner):
        xf_bf = bitb(XF)
        QTd = xf_bf[:, 0:2048]
        KTd = xf_bf[:, 2048:4096]
        Vd = xf_bf[:, 4096:6272]
        EXPS = xf_bf[:, 8192:16384]
        Yb = xf_bf[:, 16384:16896]
        YTb = xf_bf[:, 16896:17408]
        if not inner:
            dma_bcast("gpsimd", M_BCA, row(bo), "qBA")
            p.wait("vector", "qBA")
            p.wait("vector", "scal")
            for i in range(NT):
                p.op("vector", "tensor_add", XT[:, 1024 * i:1024 * (i + 1)],
                     XT[:, 1024 * i:1024 * (i + 1)], M_BCA, inc="vect")
            dma_bcast("gpsimd", M_BCB, row(bv), "qBB")
            p.wait("vector", "qBB")
        for d in range(NC_):
            if inner:
                qsrc = lambda kc: attn_w_bf[128 * kc:128 * (kc + 1), 128 * d:128 * (d + 1)]
                ksrc = lambda kc: attn_w_bf[128 * kc:128 * (kc + 1), C + 128 * d:C + 128 * (d + 1)]
                vsrc = lambda kc: attn_w_bf[128 * kc:128 * (kc + 1), 2 * C + 128 * d:2 * C + 128 * (d + 1)]
            else:
                qsrc = lambda kc: wq_bf[128 * kc:128 * (kc + 1), 128 * d:128 * (d + 1)]
                ksrc = lambda kc: wk_bf[128 * kc:128 * (kc + 1), 128 * d:128 * (d + 1)]
                vsrc = lambda kc: wv_bf[128 * kc:128 * (kc + 1), 128 * d:128 * (d + 1)]
            for which, src, DST, bcol in (("q", qsrc, QTd, 0), ("k", ksrc, KTd, 8)):
                sset = 0 if which == "q" else 1
                sem = load_units((2 * sset, 2 * sset + 1),
                                 [(qk_slot(sset, kc), src(kc)) for kc in range(NC_)])
                p.wait("tensor", sem)
                for nsl in range(4):
                    pu = 2 * sset + (nsl % 2)
                    pb = phalf(sset, nsl % 2)
                    pguard.acquire(pu, "tensor")
                    for kc in range(NC_):
                        p.op("tensor", "matmul", pb,
                             qk_slot(sset, kc),
                             CM[:, T * kc + 512 * nsl: T * kc + 512 * (nsl + 1)],
                             start=(kc == 0), stop=(kc == NC_ - 1),
                             inc="tens" if kc == NC_ - 1 else None)
                    release_units((2 * sset, 2 * sset + 1), "tens")
                    p.wait("scalar", "tens")
                    if inner:
                        p.op("scalar", "copy",
                             DST[:, 512 * nsl:512 * (nsl + 1)], pb, inc="scal")
                    else:
                        p.op("scalar", "activation",
                             DST[:, 512 * nsl:512 * (nsl + 1)], pb, AF.Identity,
                             bias=smallv[:, bcol + d:bcol + d + 1], scale=1.0,
                             inc="scal")
                    pguard.release(pu, "scal")
                if inner:
                    p.wait("tensor", "scal")
                    for nsl in range(4):
                        pu = 4
                        pb = P[2][:, 0:512]
                        pguard.acquire(pu, "tensor")
                        p.op("tensor", "matmul", pb, rmt[:],
                             DST[:, 512 * nsl:512 * (nsl + 1)],
                             start=True, stop=True, inc="tens")
                        p.wait("vector", "tens")
                        s1 = SCR[:, 0:512]
                        s2 = SCR[:, 512:1024]
                        p.op("vector", "tensor_mul", s1, pb,
                             sin_dm[:, 512 * nsl:512 * (nsl + 1)])
                        pguard.release(pu, "vect")
                        p.op("vector", "tensor_mul", s2,
                             DST[:, 512 * nsl:512 * (nsl + 1)],
                             cos_dm[:, 512 * nsl:512 * (nsl + 1)])
                        p.op("vector", "tensor_add",
                             DST[:, 512 * nsl:512 * (nsl + 1)], s1, s2, inc="vect")
                        p.wait("tensor", "vect")
            if d == 0 and DBG in ("qt", "kt"):
                dump_bf(QTd if DBG == "qt" else KTd, 2048)
                return
            sem = load_units((4, 5),
                             [(v_slot(kc), vsrc(kc)) for kc in range(NC_)])
            p.wait("tensor", sem)
            p.wait("vector", "tens")
            p.op("vector", "memset", Vd[:], 0.0, inc="vect")
            p.wait("tensor", "vect")
            for tt in range(NT):
                pu = 6 + (tt % 2)
                pb = phalf(3, tt % 2)[:, 0:128]
                pguard.acquire(pu, "tensor")
                for kc in range(NC_):
                    p.op("tensor", "matmul", pb,
                         CM[:, T * kc + 128 * tt: T * kc + 128 * (tt + 1)],
                         v_slot(kc),
                         start=(kc == 0), stop=(kc == NC_ - 1),
                         inc="tens" if kc == NC_ - 1 else None)
                p.wait("vector", "tens")
                vt = Vd[:, 136 * tt:136 * (tt + 1)]
                dst3 = vt.rearrange("p (h s) -> p h s", s=VS)[:, :, 0:64]
                src3 = pb.rearrange("p (h s) -> p h s", s=64)
                if inner:
                    p.op("vector", "tensor_copy", dst3, src3, inc="vect")
                else:
                    bc3 = M_BCB[:, 128 * d:128 * (d + 1)].rearrange(
                        "p (h s) -> p h s", s=64)
                    p.op("vector", "tensor_add", dst3, src3, bc3, inc="vect")
                pguard.release(pu, "vect")
                p.op("vector", "memset",
                     vt.rearrange("p (h s) -> p h s", s=VS)[:, :, 64:65], 1.0,
                     inc="vect")
            release_units((4, 5), "tens")
            if d == 0 and DBG == "v":
                dump_bf(Vd, 2048)
                return
            for qb in range(T // QB):
                kts = range(min(NT, 2 * qb + 2)) if inner else range(NT)
                for hh in range(2):
                    dbase = 64 * hh
                    for kt in kts:
                        pu = 2 * (kt % 2) + ((kt // 2) % 2)
                        pb = phalf(kt % 2, (kt // 2) % 2)[:, 0:QB]
                        pguard.acquire(pu, "tensor")
                        p.op("tensor", "matmul", pb,
                             KTd[dbase:dbase + HD, 128 * kt:128 * (kt + 1)],
                             QTd[dbase:dbase + HD, QB * qb:QB * (qb + 1)],
                             start=True, stop=True, inc="tens")
                        dst = EXPS[:, (hh * NT + kt) * QB:(hh * NT + kt + 1) * QB]
                        p.wait("scalar", "tens")
                        p.op("scalar", "activation", dst, pb, AF.Exp,
                             scale=0.125, inc="scal")
                        pguard.release(pu, "scal")
                        if inner and kt >= 2 * qb:
                            rel = 128 * kt - QB * qb
                            mi = 0 if rel == 0 else 1
                            p.wait("vector", "scal")
                            p.op("vector", "tensor_mul", dst, dst,
                                 cmask[:, QB * mi:QB * (mi + 1)], inc="vect")
                p.wait("tensor", "scal")
                p.wait("tensor", "vect")
                for qt2 in range(QB // 128):
                    pvkts = range(min(NT, 2 * qb + qt2 + 1)) if inner else range(NT)
                    lkt = max(pvkts)
                    for hh in range(2):
                        pu = 4
                        pb = P[2][:, 128 * hh:128 * hh + VS]
                        pguard.acquire(pu, "tensor")
                        for kt in pvkts:
                            lhs = EXPS[:, (hh * NT + kt) * QB + 128 * qt2:
                                       (hh * NT + kt) * QB + 128 * (qt2 + 1)]
                            rhs = Vd[:, 136 * kt + VS * hh:136 * kt + VS * (hh + 1)]
                            p.op("tensor", "matmul", pb, lhs, rhs,
                                 start=(kt == 0), stop=(kt == lkt),
                                 inc="tens" if kt == lkt else None)
                        p.wait("vector", "tens")
                        p.op("vector", "reciprocal", stat[:, 380:381], pb[:, 64:65],
                             inc="vect")
                        p.wait("vector", "vect")
                        p.op("vector", "tensor_scalar_mul",
                             Yb[:, 128 * qt2 + 64 * hh:128 * qt2 + 64 * (hh + 1)],
                             pb[:, 0:64], stat[:, 380:381], inc="vect")
                        pguard.release(pu, "vect")
                if d == 0 and qb == 0 and DBG == "exps":
                    dump_bf(EXPS, 8192)
                    return
                if d == 0 and qb == 0 and DBG == "y":
                    dump_bf(Yb, 256)
                    return
                p.wait("tensor", "vect")
                tguard.acquire(0, "tensor")
                for qt2 in range(QB // 128):
                    p.op("tensor", "transpose", PT[:, 128 * qt2:128 * (qt2 + 1)],
                         Yb[:, 128 * qt2:128 * (qt2 + 1)], idb[:],
                         inc="tens" if qt2 == 1 else None)
                p.wait("vector", "tens")
                p.op("vector", "tensor_copy", YTb[:, 0:256], PT[:, 0:256],
                     inc="vect")
                tguard.release(0, "vect")
                p.wait("tensor", "vect")
                wsrc = (proj_w_bf if inner else wo_bf)
                for nsl in range(2):
                    j = (d * 2 + nsl) % 2
                    sem = load_units((6 + j,),
                                     [(op_slot(j),
                                       wsrc[128 * d:128 * (d + 1),
                                            512 * nsl:512 * (nsl + 1)])])
                    p.wait("tensor", sem)
                    for qt2 in range(QB // 128):
                        tt = (QB * qb) // 128 + qt2
                        pu = 6 + (qt2 % 2)
                        pb = phalf(3, qt2 % 2)
                        pguard.acquire(pu, "tensor")
                        p.op("tensor", "matmul", pb,
                             YTb[:, 128 * qt2:128 * (qt2 + 1)], op_slot(j),
                             start=True, stop=True, inc="tens")
                        p.wait("vector", "tens")
                        xsl = XT[:, 1024 * tt + 512 * nsl:1024 * tt + 512 * (nsl + 1)]
                        p.op("vector", "tensor_add", xsl, xsl, pb, inc="vect")
                        pguard.release(pu, "vect")
                    release_units((6 + j,), "tens")
        p.barrier()


    def park(XT):
        for i in range(NT):
            p.dma("gpsimd", park_x[128 * i:128 * (i + 1), :],
                  XT[:, 1024 * i:1024 * (i + 1)], "qP")

    unp_marks = []

    def unpark_add(XF):
        for i in range(NT):
            j = i % 2
            stg = BIG[:, 1024 * j:1024 * (j + 1)]
            if i >= 2:
                p.wait("sync", "vect", unp_marks[i - 2])
            p.dma("sync", stg, park_x[128 * i:128 * (i + 1), :], f"up{j}")
            p.wait("vector", f"up{j}")
            p.op("vector", "tensor_add", XF[:, 1024 * i:1024 * (i + 1)],
                 XF[:, 1024 * i:1024 * (i + 1)], stg, inc="vect")
            unp_marks.append(p.counts["vect"])
        p.wait("scalar", "vect")
        p.wait("tensor", "vect")
        p.wait("gpsimd", "vect")

    def stage_ffn(XT, XF, scol_base):
        ln_stats(XT, scol_base)
        p.wait("gpsimd", "qO")
        for i in range(NT):
            ln_apply(XT, i, scol_base, XF[:, 1024 * i:1024 * (i + 1)],
                     inc="scal")
        p.wait("gpsimd", "scal")
        p.wait("gpsimd", "qP")
        p.wait("gpsimd", "up0")
        p.wait("gpsimd", "up1")
        park(XT)
        dma_bcast("gpsimd", M_BCA, row(ln1_w), "qBA")
        dma_bcast("gpsimd", M_BCB, row(ln1_b), "qBB")
        p.wait("vector", "qBA")
        p.wait("vector", "qBB")
        p.wait("vector", "scal")
        make_cm(XF, scol_base + 64, wb_bcast=(M_BCA, M_BCB))
        conv_cm()
        stage_attn(XF, XT, inner=True)
        dma_bcast("gpsimd", M_BCA, row(ln2_w), "qBA")
        dma_bcast("gpsimd", M_BCB, row(ln2_b), "qBB")
        p.wait("vector", "qBA")
        p.wait("vector", "qBB")
        moe(XF, XT, scol_base + 128)
        unp_marks.clear()
        unpark_add(XF)
        p.barrier()

    def moe(XF, XT, scol):
        xt_bf = bitb(XT)
        ln_stats(XF, scol)
        for i in range(NT):
            f = SCR[:, 0:1024]
            p.wait("scalar", "tens")
            p.wait("scalar", "vect")
            ln_apply(XF, i, scol, f, inc="scal")
            p.wait("vector", "scal")
            p.op("vector", "tensor_mul", f, f, M_BCA)
            p.op("vector", "tensor_add", f, f, M_BCB, inc="vect")
            p.wait("tensor", "vect")
            tguard.acquire(0, "tensor")
            for cc in range(NC_):
                p.op("tensor", "transpose", P[3][:, 128 * cc:128 * (cc + 1)],
                     f[:, 128 * cc:128 * (cc + 1)], idf[:],
                     inc="tens" if cc == NC_ - 1 else None)
            p.wait("vector", "tens")
            dst = XT.rearrange("p (cc t) -> p cc t", t=T)[:, :, 128 * i:128 * (i + 1)]
            p.op("vector", "tensor_copy", dst,
                 P[3].rearrange("p (cc r) -> p cc r", r=128), inc="vect")
            tguard.release(0, "vect")
        for cc in range(NC_):
            p.dma("sync", rw[:, 4 * cc:4 * (cc + 1)],
                  router_w[128 * cc:128 * (cc + 1), :], "qR")
        p.wait("tensor", "qR")
        p.wait("tensor", "vect")
        pguard.acquire(4, "tensor")
        for i in range(NT):
            for cc in range(NC_):
                p.op("tensor", "matmul", P[2][:, 4 * i:4 * (i + 1)],
                     XT.rearrange("p (cc t) -> p cc t", t=T)[:, cc, 128 * i:128 * (i + 1)],
                     rw[:, 4 * cc:4 * (cc + 1)],
                     start=(cc == 0), stop=(cc == NC_ - 1),
                     inc="tens" if (cc == NC_ - 1 and i == NT - 1) else None)
        p.wait("vector", "tens")
        p.op("vector", "tensor_copy", logit[:], P[2][:, 0:64], inc="vect")
        pguard.release(4, "vect")
        L = logit.rearrange("p (t e) -> p t e", e=4)
        Me = emask.rearrange("p (t e) -> p t e", e=4)
        mx = stat[:, 368:384]
        run = stat[:, 352:368]

        def vop(*a, **k):
            p.op("vector", *a, inc="vect", **k)
            p.wait("vector", "vect")

        p.wait("vector", "vect")
        vop("tensor_max", mx, L[:, :, 0], L[:, :, 1])
        vop("tensor_max", mx, mx, L[:, :, 2])
        vop("tensor_max", mx, mx, L[:, :, 3])
        vop("memset", run, 1.0)
        for e in range(NEXP):
            ge = stat[:, 336:352]
            vop("tensor_tensor", ge, L[:, :, e], mx, ALU.is_ge)
            vop("tensor_mul", Me[:, :, e], ge, run)
            if e < NEXP - 1:
                vop("tensor_scalar", ge, ge, -1.0, 1.0, ALU.mult, ALU.add)
                vop("tensor_mul", run, run, ge)
        p.wait("vector", "tens")
        for cc in range(NC_):
            p.op("vector", "tensor_copy", CM[:, 2048 * cc:2048 * (cc + 1)],
                 XT[:, 2048 * cc:2048 * (cc + 1)],
                 inc="vect" if cc == NC_ - 1 else None)
        p.wait("tensor", "vect")
        p.wait("scalar", "vect")
        glu = xt_bf
        wv_r = exp_wv_bf.rearrange("(e c p) n -> e p c n", e=NEXP, p=128)
        pj_r = exp_proj_bf.rearrange("(e c p) n -> e p c n", e=NEXP, p=128)
        for e in range(NEXP):
            for tg in range(2):
                tok0 = 1024 * tg
                for m in range(32):
                    j = m % 2
                    gof = 2048 * j
                    gsl = load_units((4 * j, 4 * j + 1), [(
                        pool_bf[:, gof:gof + 1024].rearrange(
                            "p (c n) -> p c n", n=128),
                        wv_r[e][:, :, 128 * m:128 * (m + 1)])])
                    vsl = load_units((4 * j + 2, 4 * j + 3), [(
                        pool_bf[:, gof + 1024:gof + 2048].rearrange(
                            "p (c n) -> p c n", n=128),
                        wv_r[e][:, :, DFF + 128 * m:DFF + 128 * (m + 1)])])
                    p.wait("tensor", gsl)
                    p.wait("tensor", vsl)
                    for ns in range(2):
                        pg = phalf(0, ns)
                        pv_ = phalf(1, ns)
                        pguard.acquire(0 + ns, "tensor")
                        pguard.acquire(2 + ns, "tensor")
                        for kc in range(NC_):
                            p.op("tensor", "matmul", pg,
                                 pool_bf[:, gof + 128 * kc:gof + 128 * (kc + 1)],
                                 CM[:, T * kc + tok0 + 512 * ns:
                                    T * kc + tok0 + 512 * (ns + 1)],
                                 start=(kc == 0), stop=(kc == NC_ - 1))
                        for kc in range(NC_):
                            p.op("tensor", "matmul", pv_,
                                 pool_bf[:, gof + 1024 + 128 * kc:
                                         gof + 1024 + 128 * (kc + 1)],
                                 CM[:, T * kc + tok0 + 512 * ns:
                                    T * kc + tok0 + 512 * (ns + 1)],
                                 start=(kc == 0), stop=(kc == NC_ - 1),
                                 inc="tens" if kc == NC_ - 1 else None)
                        p.wait("scalar", "tens")
                        sg = SCR[:, 0:512]
                        p.op("scalar", "activation", sg, pg, AF.Silu, inc="scal")
                        p.wait("vector", "scal")
                        p.op("vector", "tensor_mul",
                             glu[:, 1024 * m + 512 * ns:1024 * m + 512 * (ns + 1)],
                             sg, pv_, inc="vect")
                        pguard.release(0 + ns, "vect")
                        pguard.release(2 + ns, "vect")
                    release_units((4 * j, 4 * j + 1, 4 * j + 2, 4 * j + 3), "tens")
                p.wait("tensor", "vect")
                for tgrp in range(2):
                    for nsl in range(2):
                        pus = [0, 1, 2, 3]
                        for u in pus:
                            pguard.acquire(u, "tensor")
                        for g8 in range(8):
                            j = g8 % 2
                            sem = load_units((4 * j, 4 * j + 1, 4 * j + 2, 4 * j + 3), [(
                                pool_bf[:, 2048 * j:2048 * j + 2048].rearrange(
                                    "p (c n) -> p c n", n=512),
                                pj_r[e][:, 4 * g8:4 * (g8 + 1),
                                        512 * nsl:512 * (nsl + 1)])])
                            p.wait("tensor", sem)
                            for t4 in range(4):
                                tt2 = 4 * tgrp + t4
                                pb = phalf(t4 // 2, t4 % 2)
                                for jj in range(4):
                                    kc = 4 * g8 + jj
                                    p.op("tensor", "matmul", pb,
                                         glu[:, 1024 * kc + 128 * tt2:
                                             1024 * kc + 128 * (tt2 + 1)],
                                         pool_bf[:, 2048 * j + 512 * jj:
                                                 2048 * j + 512 * (jj + 1)],
                                         start=(kc == 0), stop=(kc == 31),
                                         inc="tens" if (jj == 3 and t4 == 3)
                                         else None)
                            release_units((4 * j, 4 * j + 1, 4 * j + 2, 4 * j + 3), "tens")
                        p.wait("vector", "tens")
                        for t4 in range(4):
                            tt2 = 4 * tgrp + t4
                            tt = 8 * tg + tt2
                            pb = phalf(t4 // 2, t4 % 2)
                            p.op("vector", "scalar_tensor_tensor",
                                 XF[:, 1024 * tt + 512 * nsl:
                                    1024 * tt + 512 * (nsl + 1)],
                                 pb, emask[:, 4 * tt + e:4 * tt + e + 1],
                                 XF[:, 1024 * tt + 512 * nsl:
                                    1024 * tt + 512 * (nsl + 1)],
                                 ALU.mult, ALU.add,
                                 inc="vect" if t4 == 3 else None)
                        for u in pus:
                            pguard.release(u, "vect")

    def stage_ssm(XT, XF, scol):
        ln_stats(XT, scol)
        dma_bcast("gpsimd", M_BCA, row(D_par), "qBA")
        p.wait("vector", "qBA")
        for i in range(NT):
            f = SCR[:, 0:1024]
            p.wait("scalar", "tens")
            p.wait("scalar", "vect")
            ln_apply(XT, i, scol, f, inc="scal")
            p.wait("tensor", "scal")
            tguard.acquire(0, "tensor")
            for cc in range(NC_):
                p.op("tensor", "transpose", P[3][:, 128 * cc:128 * (cc + 1)],
                     f[:, 128 * cc:128 * (cc + 1)], idf[:],
                     inc="tens" if cc == NC_ - 1 else None)
            p.wait("vector", "scal")
            ud = BIG[:, 7168:8192]
            p.op("vector", "tensor_mul", ud, f, M_BCA)
            p.op("vector", "tensor_add", XT[:, 1024 * i:1024 * (i + 1)],
                 XT[:, 1024 * i:1024 * (i + 1)], ud)
            p.wait("vector", "tens")
            dst = XF.rearrange("p (cc t) -> p cc t", t=T)[:, :, 128 * i:128 * (i + 1)]
            p.op("vector", "tensor_copy", dst,
                 P[3].rearrange("p (cc r) -> p cc r", r=128), inc="vect")
            tguard.release(0, "vect")
        xdbl = POOL[0:96, :]
        p.wait("tensor", "vect")
        for sl4 in range(4):
            pguard.acquire(4, "tensor")
            for cc in range(NC_):
                p.op("tensor", "matmul", P[2][0:96, 0:512],
                     wxt[:, 96 * cc:96 * (cc + 1)],
                     XF[:, 2048 * cc + 512 * sl4:2048 * cc + 512 * (sl4 + 1)],
                     start=(cc == 0), stop=(cc == NC_ - 1),
                     inc="tens" if cc == NC_ - 1 else None)
            p.wait("vector", "tens")
            p.op("vector", "tensor_copy", xdbl[:, 512 * sl4:512 * (sl4 + 1)],
                 P[2][0:96, 0:512], inc="vect")
            pguard.release(4, "vect")
        p.wait("gpsimd", "vect")
        p.dma("gpsimd", bc_rows[:], xdbl[64:96, :], "qBC")
        bc_guard = [0]
        DT_ = BIG[:, 0:1024]
        DTU = BIG[:, 1024:2048]
        ABR = BIG[:, 2048:3072]
        HH_ = BIG[:, 3072:4096]
        TT_ = BIG[:, 2048:3072]
        BBC = BIG[:, 4096:5120]
        CBC = BIG[:, 5120:6144]
        YH = BIG[:, 6144:7168]
        for lh in range(2):
            l0 = 1024 * lh
            for d in range(NC_):
                for s2 in range(2):
                    pguard.acquire(4, "tensor")
                    p.op("tensor", "matmul", P[2][:, 0:512],
                         wdtt[:, 128 * d:128 * (d + 1)],
                         xdbl[0:64, l0 + 512 * s2:l0 + 512 * (s2 + 1)],
                         start=True, stop=True, inc="tens")
                    bdt = smallv[:, 16 + d:17 + d]
                    p.wait("scalar", "tens")
                    sl_ = slice(512 * s2, 512 * (s2 + 1))
                    p.op("scalar", "activation", DT_[:, sl_], P[2][:, 0:512],
                         AF.Relu, bias=bdt, scale=1.0)
                    p.op("scalar", "activation", SCR[:, 0:512], P[2][:, 0:512],
                         AF.Abs, bias=bdt, scale=1.0)
                    p.op("scalar", "activation", SCR[:, 0:512], SCR[:, 0:512],
                         AF.Exp, scale=-1.0)
                    p.op("scalar", "activation", SCR[:, 0:512], SCR[:, 0:512],
                         AF.Ln, bias=1.0, inc="scal")
                    p.wait("vector", "scal")
                    p.op("vector", "tensor_add", DT_[:, sl_], DT_[:, sl_],
                         SCR[:, 0:512], inc="vect")
                    pguard.release(4, "vect")
                p.op("vector", "tensor_mul", DTU, DT_,
                     XF[:, 2048 * d + l0:2048 * d + l0 + 1024], inc="vect")
                for n in range(16):
                    p.wait("sync", "qBC")
                    if True:
                        p.wait("sync", "vect", bc_guard[0])
                        p.dma("sync", BBC,
                              bc_rows[n:n + 1, l0:l0 + 1024].partition_broadcast(128),
                              "qBB2")
                        p.dma("sync", CBC,
                              bc_rows[16 + n:17 + n, l0:l0 + 1024].partition_broadcast(128),
                              "qBB2")
                    A_col = smallv[:, 24 + 16 * d + n:25 + 16 * d + n]
                    p.wait("scalar", "vect")
                    p.op("scalar", "activation", ABR, DT_, AF.Exp, scale=A_col,
                         inc="scal")
                    p.wait("vector", "qBB2")
                    p.wait("vector", "scal")
                    p.op("vector", "tensor_mul", SCR[:, 0:1024], DTU, BBC)
                    p.op("vector", "tensor_tensor_scan", HH_, ABR, SCR[:, 0:1024],
                         0.0 if lh == 0 else carry[:, 16 * d + n:16 * d + n + 1],
                         ALU.mult, ALU.add, inc="vect")
                    p.wait("vector", "vect")
                    p.op("vector", "tensor_copy", carry[:, 16 * d + n:16 * d + n + 1],
                         HH_[:, 1023:1024])
                    p.op("vector", "tensor_mul", TT_, HH_, CBC, inc="vect")
                    bc_guard[0] = p.counts["vect"]
                    if n == 0:
                        p.op("vector", "tensor_copy", YH, TT_, inc="vect")
                    else:
                        p.op("vector", "tensor_add", YH, YH, TT_, inc="vect")
                p.wait("tensor", "vect")
                tguard.acquire(0, "tensor")
                for j in range(NC_):
                    p.op("tensor", "transpose", P[3][:, 128 * j:128 * (j + 1)],
                         YH[:, 128 * j:128 * (j + 1)], idf[:],
                         inc="tens" if j == NC_ - 1 else None)
                p.wait("vector", "tens")
                dstx = XT.rearrange("p (tt t) -> p tt t", t=1024)[
                    :, 8 * lh:8 * lh + 8, 128 * d:128 * (d + 1)]
                p.op("vector", "tensor_add", dstx, dstx,
                     P[3].rearrange("p (j r) -> p j r", r=128), inc="vect")
                tguard.release(0, "vect")
        p.barrier()
    with es:
        load_consts()
        prologue_weights()
        bufs = [XA, XB]
        for b in range(BPC):
            load_x(b, bufs[0])
            for e in ("vector", "scalar", "tensor", "gpsimd"):
                p.wait(e, "qX")
            if DBG == "xa" and b == 0:
                dumped[0] = True
                p.barrier()
                for i in range(NT):
                    p.dma("gpsimd", x_out[128 * i:128 * (i + 1), :],
                          bufs[0][:, 1024 * i:1024 * (i + 1)], "qO")
                p.wait("gpsimd", "qO")
            if DBG in ("wq", "wq16") and b == 0:
                p.barrier()
                for r in range(NC_):
                    p.dma("sync", big_bf[:, 1024 * r:1024 * (r + 1)],
                          wq_bf[128 * r:128 * (r + 1), :], "qX")
                if DBG == "wq16":
                    for r in range(NC_):
                        p.dma("sync", big_bf[:, 8192 + 1024 * r:8192 + 1024 * (r + 1)],
                              wk_bf[128 * r:128 * (r + 1), :], "qX")
                p.wait("scalar", "qX")
                dump_bf(big_bf[:, 0:8192] if DBG == "wq" else big_bf[:, 0:16384],
                        8192 if DBG == "wq" else 16384)
            xcur, xfree = bufs[0], bufs[1]
            if STAGES >= 1:
                make_cm(xcur, 0)
                if DBG == "cm" and b == 0:
                    dump_bf(CM, NC_ * T)
                if not dumped[0]:
                    stage_attn(xcur, xfree, inner=False)
            if STAGES >= 2 and not dumped[0]:
                stage_ffn(xcur, xfree, 64)
                xcur, xfree = xfree, xcur
            if STAGES >= 3 and not dumped[0]:
                stage_ssm(xcur, xfree, 0)
            if STAGES >= 4 and not dumped[0]:
                stage_ffn(xcur, xfree, 64)
                xcur, xfree = xfree, xcur
            if not dumped[0]:
                store_x(b, xcur)
            if dumped[0]:
                break
        p.wait("gpsimd", "qO")
        p.barrier()

        with nc.Block() as block:
            for ename in ENGINES:
                def make(en):
                    def _(eng):
                        for fn in p.streams[en]:
                            fn(eng)
                    return _
                getattr(block, ename)(make(ename))
    return nc


def _constants():
    ident = np.eye(128, dtype=np.float32)
    ident_b = np.eye(128).astype(ml_dtypes.bfloat16)
    inv_freq = 1.0 / (10000.0 ** (np.arange(0, HD, 2, dtype=np.float64) / HD))
    t = np.arange(T, dtype=np.float64)
    freqs = np.einsum("i,j->ij", t, inv_freq)
    emb = np.concatenate([freqs, freqs], axis=-1)
    cos = np.cos(emb)
    sin = np.sin(emb)
    cos_dm = np.concatenate([cos.T, cos.T], axis=0).astype(ml_dtypes.bfloat16)
    sin_dm = np.concatenate([sin.T, sin.T], axis=0).astype(ml_dtypes.bfloat16)
    R = np.zeros((128, 128), dtype=np.float64)
    for hh in range(2):
        o = 64 * hh
        for dd in range(32):
            R[o + dd, o + dd + 32] = -1.0
            R[o + dd + 32, o + dd] = 1.0
    rmat = R.T.astype(ml_dtypes.bfloat16)
    kk = np.arange(128)[:, None]
    qq = np.arange(QB)[None, :]
    masks = np.concatenate(
        [((rel + kk) <= qq).astype(np.float64) for rel in (0, 128)],
        axis=1).astype(ml_dtypes.bfloat16)
    return ident, ident_b, cos_dm, sin_dm, rmat, masks


def kernel(**inputs):
    g = lambda k: np.ascontiguousarray(np.asarray(inputs[k], dtype=np.float32))
    x = g("L0_attn_a_x")
    ident, ident_b, cos_dm, sin_dm, rmat, masks = _constants()
    base = dict(
        Wq=g("L0_attn_a_Wq"), bq=g("L0_attn_a_bq"), Wk=g("L0_attn_a_Wk"),
        bk=g("L0_attn_a_bk"), Wv=g("L0_attn_a_Wv"), bv=g("L0_attn_a_bv"),
        Wo=g("L0_attn_a_Wo"), bo=g("L0_attn_a_bo"),
        A_log=g("L0_attn_b_A_log"), W_x=g("L0_attn_b_W_x"),
        W_dt=g("L0_attn_b_W_dt"), b_dt=g("L0_attn_b_b_dt"),
        D_par=g("L0_attn_b_D_param"),
        ln1_w=g("L0_ffn_ln1_w"), ln1_b=g("L0_ffn_ln1_b"),
        ln2_w=g("L0_ffn_ln2_w"), ln2_b=g("L0_ffn_ln2_b"),
        conv_w=g("L0_ffn_conv_w").reshape(C, 3),
        attn_w=g("L0_ffn_attn_w"), proj_w=g("L0_ffn_proj_w"),
        router_w=g("L0_ffn_router_w"),
        exp_wv=g("L0_ffn_expert_wv").reshape(NEXP * C, 2 * DFF),
        exp_proj=g("L0_ffn_expert_proj").reshape(NEXP * DFF, C),
        ident_f=ident, ident_b=ident_b, rmat=rmat,
        cos_dm_i=cos_dm, sin_dm_i=sin_dm, masks=masks,
    )
    in_maps = []
    for c in range(NCORES):
        m = dict(base)
        m["x_in"] = np.ascontiguousarray(
            x[BPC * c:BPC * (c + 1)].reshape(BPC * T, C))
        in_maps.append(m)
    nc = build_program()
    res = run_bass_kernel_spmd(nc, in_maps, core_ids=list(range(NCORES)))
    out = np.concatenate(
        [r["x_out"].reshape(BPC, T, C) for r in res.results], axis=0)
    return out.astype(np.float32)




# version 04
# speedup vs baseline: 2.3836x; 1.0419x over previous
import numpy as np
import ml_dtypes
import concourse.bass as bass
import concourse.mybir as mybir
from concourse.bass_utils import run_bass_kernel_spmd

F32 = mybir.dt.float32
BF16 = mybir.dt.bfloat16
AF = mybir.ActivationFunctionType
ALU = mybir.AluOpType
AX = mybir.AxisListType

B_FULL, T, C = 16, 2048, 1024
NCORES = 8
BPC = B_FULL // NCORES
NT = T // 128
NC_ = C // 128
H, HD = 16, 64
EPS = 1e-5
DFF = 4096
NEXP = 4
QB = 256
VS = 68

ENGINES = ("sync", "gpsimd", "tensor", "vector", "scalar")
STAGES = 4
DBG = ""


class Prog:
    def __init__(self, nc):
        self.nc = nc
        self.streams = {e: [] for e in ENGINES}
        self.sems = {}
        self.counts = {}
        self.waited = {e: {} for e in ENGINES}

    def sem(self, name):
        s = self.nc.semaphore(name).__enter__()
        self.sems[name] = s
        self.counts[name] = 0

    def emit(self, eng, fn):
        self.streams[eng].append(fn)

    def dma(self, eng, out, in_, sem_name, **kw):
        if sem_name not in self.sems:
            self.sem(sem_name)
        s = self.sems[sem_name]
        self.counts[sem_name] += 16
        self.emit(eng, lambda e: e.dma_start(out=out, in_=in_, **kw).then_inc(s, 16))

    def op(self, eng, method, *args, inc=None, **kw):
        if inc is None:
            self.emit(eng, lambda e: getattr(e, method)(*args, **kw))
        else:
            s = self.sems[inc]
            self.counts[inc] += 1
            self.emit(eng, lambda e: getattr(e, method)(*args, **kw).then_inc(s, 1))

    def wait(self, eng, sem_name, val=None):
        if sem_name not in self.counts:
            return
        if val is None:
            val = self.counts[sem_name]
        if val <= 0:
            return
        if val <= self.waited[eng].get(sem_name, 0):
            return
        self.waited[eng][sem_name] = val
        s = self.sems[sem_name]
        self.emit(eng, lambda e: e.wait_ge(s, val))

    def barrier(self):
        for e in ENGINES:
            s = self.sems["b_" + e]
            self.counts["b_" + e] += 1
            self.emit(e, lambda en, s=s: en.sem_inc(s, 1))
        for e in ENGINES:
            for o in ENGINES:
                self.wait(e, "b_" + o)


class Guard:
    def __init__(self, p, n):
        self.p = p
        self.g = [None] * n

    def acquire(self, i, eng):
        if self.g[i] is not None:
            self.p.wait(eng, self.g[i][0], self.g[i][1])

    def release(self, i, sem_name):
        self.g[i] = (sem_name, self.p.counts[sem_name])


def build_program(detect_races=True):
    nc = bass.Bass("TRN2", target_bir_lowering=False,
                   detect_race_conditions=detect_races)
    p = Prog(nc)
    for name in ("qS", "qG", "tens", "vect", "scal", "pool"):
        p.sem(name)
    for e in ENGINES:
        p.sem("b_" + e)

    dt_f = lambda n, sh: nc.dram_tensor(n, list(sh), F32, kind="ExternalInput").ap()
    dt_b = lambda n, sh: nc.dram_tensor(n, list(sh), BF16, kind="ExternalInput").ap()

    x_in = dt_f("x_in", (BPC * T, C))
    Wq = dt_f("Wq", (C, C)); bq = dt_f("bq", (C,))
    Wk = dt_f("Wk", (C, C)); bk = dt_f("bk", (C,))
    Wv = dt_f("Wv", (C, C)); bv = dt_f("bv", (C,))
    Wo = dt_f("Wo", (C, C)); bo = dt_f("bo", (C,))
    A_log = dt_f("A_log", (C, 16))
    W_x = dt_f("W_x", (96, C))
    W_dt = dt_f("W_dt", (C, 64))
    b_dt = dt_f("b_dt", (C,))
    D_par = dt_f("D_par", (C,))
    ln1_w = dt_f("ln1_w", (C,)); ln1_b = dt_f("ln1_b", (C,))
    ln2_w = dt_f("ln2_w", (C,)); ln2_b = dt_f("ln2_b", (C,))
    conv_w = dt_f("conv_w", (C, 3))
    attn_w = dt_f("attn_w", (C, 3 * C))
    proj_w = dt_f("proj_w", (C, C))
    router_w = dt_f("router_w", (C, NEXP))
    exp_wv = dt_f("exp_wv", (NEXP * C, 2 * DFF))
    exp_proj = dt_f("exp_proj", (NEXP * DFF, C))
    ident_f = dt_f("ident_f", (128, 128))
    ident_b = dt_b("ident_b", (128, 128))
    rmat = dt_b("rmat", (128, 128))
    cos_dm_in = dt_b("cos_dm_i", (128, T))
    sin_dm_in = dt_b("sin_dm_i", (128, T))
    masks_in = dt_b("masks", (128, 2 * QB))

    x_out = nc.dram_tensor("x_out", [BPC * T, C], F32, kind="ExternalOutput").ap()

    dram_i = lambda n, sh, dt: nc.dram_tensor(n, list(sh), dt, kind="Internal").ap()
    wq_bf = dram_i("wq_bf", (C, C), BF16)
    wk_bf = dram_i("wk_bf", (C, C), BF16)
    wv_bf = dram_i("wv_bf", (C, C), BF16)
    wo_bf = dram_i("wo_bf", (C, C), BF16)
    attn_w_bf = dram_i("attn_w_bf", (C, 3 * C), BF16)
    proj_w_bf = dram_i("proj_w_bf", (C, C), BF16)
    exp_wv_bf = dram_i("exp_wv_bf", (NEXP * C, 2 * DFF), BF16)
    exp_proj_bf = dram_i("exp_proj_bf", (NEXP * DFF, C), BF16)
    park_x = dram_i("park_x", (T, C), F32)
    bc_rows = dram_i("bc_rows", (32, T), F32)

    from contextlib import ExitStack
    es = ExitStack()
    sb = lambda n, sh, dt: es.enter_context(nc.sbuf_tensor(n, list(sh), dt))
    XA = sb("XA", (128, NT * 1024), F32)
    XB = sb("XB", (128, NT * 1024), F32)
    BIG = sb("BIG", (128, 9728), F32)
    POOL = sb("POOL", (128, 2048), F32)
    MISC = sb("MISC", (128, 3072), F32)
    idf = sb("idf", (128, 128), F32)
    idb = sb("idb", (128, 128), BF16)
    rmt = sb("rmt", (128, 128), BF16)
    cos_dm = sb("cos_dm", (128, T), BF16)
    sin_dm = sb("sin_dm", (128, T), BF16)
    stat = sb("stat", (128, 384), F32)
    smallv = sb("smallv", (128, 256), F32)
    wxt = sb("wxt", (128, 768), F32)
    wdtt = sb("wdtt", (64, 1024), F32)
    cmask = sb("cmask", (128, 2 * QB), BF16)
    carry = sb("carry", (128, 128), F32)
    logit = sb("logit", (128, 64), F32)
    emask = sb("emask", (128, 64), F32)
    rw = sb("rw", (128, 32), F32)
    P = [es.enter_context(nc.psum_tensor("P0", [128, 1024], F32)),
         es.enter_context(nc.psum_tensor("P1", [128, 1024], F32)),
         es.enter_context(nc.psum_tensor("P2", [128, 512], F32)),
         es.enter_context(nc.psum_tensor("P3", [128, 1024], F32))]
    PT = es.enter_context(nc.psum_tensor("PT", [128, 1024], BF16))

    M_BCA = MISC[:, 0:1024]
    M_BCB = MISC[:, 1024:2048]
    M_SS = MISC[:, 2048:3072]

    bitb = lambda ap: ap.bitcast(BF16)
    pool_bf = bitb(POOL)
    big_bf = bitb(BIG)

    CM = big_bf[:, 0:NC_ * T]
    SCR = BIG[:, 8192:9728]
    BNC = big_bf[:, 18432:19456]

    slots = Guard(p, 8)

    def punit(u):
        return pool_bf[:, 512 * u:512 * (u + 1)]

    def qk_slot(sset, kc):
        return pool_bf[:, 128 * (8 * sset + kc):128 * (8 * sset + kc) + 128]

    def v_slot(kc):
        return pool_bf[:, 2048 + 128 * kc:2048 + 128 * (kc + 1)]

    def op_slot(j):
        return pool_bf[:, 3072 + 512 * j:3072 + 512 * (j + 1)]

    def load_units(units, loads):
        for u in units:
            slots.acquire(u, "sync")
        sem = f"pu{units[0]}"
        for dst, srcap in loads:
            p.dma("sync", dst, srcap, sem)
        return sem

    def release_units(units, sem_name):
        for u in units:
            slots.release(u, sem_name)

    pguard = Guard(p, 8)

    def phalf(i, j):
        return P[i][:, 512 * j:512 * (j + 1)]

    dumped = [False]

    def dump_bf(ap_bf, ncols):
        if dumped[0]:
            return
        dumped[0] = True
        p.barrier()
        nblk = ncols // 1024
        for i in range(nblk):
            p.op("scalar", "copy", SCR[:, 0:1024], ap_bf[:, 1024 * i:1024 * (i + 1)],
                 inc="scal")
            p.wait("gpsimd", "scal")
            p.dma("gpsimd", x_out[128 * i:128 * (i + 1), :], SCR[:, 0:1024], "qO")
            p.wait("scalar", "qO")
        p.barrier()

    def load_consts():
        p.dma("sync", idf[:], ident_f, "qC")
        p.dma("sync", idb[:], ident_b, "qC")
        p.dma("sync", rmt[:], rmat, "qC")
        p.dma("sync", cos_dm[:], cos_dm_in, "qC")
        p.dma("sync", sin_dm[:], sin_dm_in, "qC")
        p.dma("sync", cmask[:], masks_in, "qC")
        for cc in range(NC_):
            sl = slice(128 * cc, 128 * (cc + 1))
            p.dma("sync", smallv[:, cc:cc + 1], bq[sl].rearrange("(p o) -> p o", o=1), "qC")
            p.dma("sync", smallv[:, 8 + cc:9 + cc], bk[sl].rearrange("(p o) -> p o", o=1), "qC")
            p.dma("sync", smallv[:, 16 + cc:17 + cc], b_dt[sl].rearrange("(p o) -> p o", o=1), "qC")
            p.dma("sync", smallv[:, 24 + 16 * cc:24 + 16 * (cc + 1)], A_log[sl, :], "qC")
            p.dma("sync", smallv[:, 160 + 3 * cc:163 + 3 * cc], conv_w[sl, :], "qC")
        p.wait("scalar", "qC")
        p.op("scalar", "activation", smallv[:, 24:152], smallv[:, 24:152], AF.Exp)
        p.op("scalar", "mul", smallv[:, 24:152], smallv[:, 24:152], -1.0, inc="scal")

    def prologue_weights():
        jobs = [(Wq, wq_bf, C, C), (Wk, wk_bf, C, C), (Wv, wv_bf, C, C),
                (Wo, wo_bf, C, C), (attn_w, attn_w_bf, C, 3 * C),
                (proj_w, proj_w_bf, C, C),
                (exp_wv, exp_wv_bf, NEXP * C, 2 * DFF),
                (exp_proj, exp_proj_bf, NEXP * DFF, C)]
        inb = Guard(p, 2)
        outb = Guard(p, 2)
        nb = 0
        for src, dst, rows, cols in jobs:
            for r in range(rows // 128):
                for c0 in range(0, cols, 1024):
                    w = min(1024, cols - c0)
                    i = nb % 2
                    buf = POOL[:, 1024 * i:1024 * i + w]
                    ob = big_bf[:, 16384 + 1024 * i:16384 + 1024 * i + w]
                    inb.acquire(i, "sync")
                    p.dma("sync", buf, src[128 * r:128 * (r + 1), c0:c0 + w], f"pi{i}")
                    outb.acquire(i, "vector")
                    p.wait("vector", f"pi{i}")
                    p.op("vector", "tensor_copy", ob, buf, inc="vect")
                    inb.release(i, "vect")
                    p.wait("gpsimd", "vect")
                    p.dma("gpsimd", dst[128 * r:128 * (r + 1), c0:c0 + w], ob, f"po{i}")
                    outb.release(i, f"po{i}")
                    nb += 1
        WXS = BIG[:, 0:1024]
        p.wait("vector", "po0")
        p.wait("vector", "po1")
        p.op("vector", "memset", WXS, 0.0, inc="vect")
        p.wait("sync", "vect")
        p.dma("sync", WXS[0:96, :], W_x, "qW")
        p.wait("tensor", "qW")
        for cc in range(NC_):
            p.op("tensor", "transpose", P[0][:, 0:128], WXS[:, 128 * cc:128 * (cc + 1)],
                 idf[:], inc="tens")
            p.wait("scalar", "tens")
            p.op("scalar", "copy", wxt[:, 96 * cc:96 * (cc + 1)], P[0][:, 0:96], inc="scal")
            p.wait("tensor", "scal")
        p.wait("sync", "scal")
        p.wait("sync", "qW")
        for cc in range(NC_):
            p.dma("sync", BIG[:, 1024 + 64 * cc:1024 + 64 * (cc + 1)],
                  W_dt[128 * cc:128 * (cc + 1), :], "qW")
        p.wait("tensor", "qW")
        for cc in range(NC_):
            p.op("tensor", "transpose", P[0][0:64, 128:256],
                 BIG[:, 1024 + 64 * cc:1024 + 64 * (cc + 1)], idf[:], inc="tens")
            p.wait("scalar", "tens")
            p.op("scalar", "copy", wdtt[:, 128 * cc:128 * (cc + 1)], P[0][0:64, 128:256],
                 inc="scal")
            p.wait("tensor", "scal")
        p.barrier()

    def dma_bcast(eng, dst_tile, src_row_ap, q):
        p.dma(eng, dst_tile, src_row_ap.partition_broadcast(128), q)

    def row(v):
        return v.rearrange("(o c) -> o c", o=1)

    def load_x(b, XT):
        p.wait("sync", "qO")
        for i in range(NT):
            p.dma("sync", XT[:, 1024 * i:1024 * (i + 1)],
                  x_in[b * T + 128 * i: b * T + 128 * (i + 1), :], "qX")

    def store_x(b, XT):
        p.wait("gpsimd", "vect")
        p.wait("gpsimd", "scal")
        for i in range(NT):
            p.dma("gpsimd", x_out[b * T + 128 * i: b * T + 128 * (i + 1), :],
                  XT[:, 1024 * i:1024 * (i + 1)], "qO")

    def ln_stats(XT, scol):
        mu = stat[:, scol:scol + 16]
        rs = stat[:, scol + 16:scol + 32]
        nm = stat[:, scol + 32:scol + 48]
        sq = stat[:, scol + 48:scol + 64]
        for i in range(NT):
            xt = XT[:, 1024 * i:1024 * (i + 1)]
            p.op("scalar", "activation", bitb(M_SS)[:, 0:1024], xt, AF.Square,
                 accum_out=sq[:, i:i + 1],
                 inc="scal" if i == NT - 1 else None)
            p.op("vector", "reduce_sum", mu[:, i:i + 1], xt, AX.X,
                 inc="vect" if i == NT - 1 else None)
        p.wait("scalar", "scal")
        p.wait("vector", "vect")
        p.op("scalar", "mul", sq, sq, 1.0 / C, inc="scal")
        p.op("vector", "tensor_scalar_mul", mu, mu, 1.0 / C, inc="vect")
        p.wait("vector", "scal")
        p.wait("vector", "vect")
        p.op("vector", "tensor_mul", nm, mu, mu, inc="vect")
        p.wait("vector", "vect")
        p.op("vector", "tensor_sub", rs, sq, nm, inc="vect")
        p.wait("vector", "vect")
        p.op("vector", "tensor_scalar_add", rs, rs, EPS, inc="vect")
        p.wait("scalar", "vect")
        p.op("scalar", "activation", rs, rs, AF.Ln)
        p.op("scalar", "activation", rs, rs, AF.Exp, scale=-0.5, inc="scal")
        p.wait("vector", "scal")
        p.op("vector", "tensor_mul", nm, mu, rs, inc="vect")
        p.wait("vector", "vect")
        p.op("vector", "tensor_scalar_mul", nm, nm, -1.0, inc="vect")
        p.wait("scalar", "vect")

    def ln_apply(XT, i, scol, out_ap, inc=None):
        rs = stat[:, scol + 16 + i:scol + 17 + i]
        nm = stat[:, scol + 32 + i:scol + 33 + i]
        p.op("scalar", "activation", out_ap, XT[:, 1024 * i:1024 * (i + 1)],
             AF.Identity, bias=nm, scale=rs, inc=inc)

    tguard = Guard(p, 1)

    def transpose_tile_to_cm(src_bf_tile, dst_cm_bf, i, T_cols, nch=NC_):
        tguard.acquire(0, "tensor")
        for cc in range(nch):
            p.op("tensor", "transpose", PT[:, 128 * cc:128 * (cc + 1)],
                 src_bf_tile[:, 128 * cc:128 * (cc + 1)], idb[:],
                 inc="tens" if cc == nch - 1 else None)
        p.wait("vector", "tens")
        dst = dst_cm_bf.rearrange("p (cc t) -> p cc t", t=T_cols)[
            :, 0:nch, 128 * i:128 * (i + 1)]
        p.op("vector", "tensor_copy", dst,
             PT[:, 0:128 * nch].rearrange("p (cc r) -> p cc r", r=128), inc="vect")
        tguard.release(0, "vect")

    def make_cm(XT, scol, wb_bcast=None):
        ln_stats(XT, scol)
        if DBG == "stats" and not dumped[0]:
            dumped[0] = True
            p.barrier()
            p.dma("gpsimd", x_out[0:128, 0:384], stat[:], "qO")
            p.wait("gpsimd", "qO")
            return
        ntiles = NT
        if DBG.startswith("cmn") and not dumped[0]:
            k = int(DBG[3:])
            for i in range(k):
                stg = bitb(M_SS)[:, 0:1024]
                p.wait("scalar", "tens")
                ln_apply(XT, i, scol, stg, inc="scal")
                p.wait("tensor", "scal")
                transpose_tile_to_cm(stg, CM, i, T)
            dump_bf(CM, 2048 * k if k <= 4 else NC_ * T)
            return
        if DBG == "stage" and not dumped[0]:
            dumped[0] = True
            stg = bitb(M_SS)[:, 0:1024]
            p.wait("scalar", "tens")
            ln_apply(XT, 0, scol, stg, inc="scal")
            p.barrier()
            p.op("scalar", "copy", SCR[:, 0:1024], stg, inc="scal")
            p.wait("gpsimd", "scal")
            p.dma("gpsimd", x_out[0:128, :], SCR[:, 0:1024], "qO")
            p.wait("gpsimd", "qO")
            return
        for i in range(NT):
            stg = bitb(M_SS)[:, 0:1024]
            p.wait("scalar", "tens")
            if wb_bcast is None:
                ln_apply(XT, i, scol, stg, inc="scal")
                p.wait("tensor", "scal")
                transpose_tile_to_cm(stg, CM, i, T)
            else:
                wbc, bbc = wb_bcast
                f = SCR[:, 0:1024]
                ln_apply(XT, i, scol, f, inc="scal")
                p.wait("vector", "scal")
                p.op("vector", "tensor_mul", f, f, wbc)
                p.op("vector", "tensor_add", stg, f, bbc, inc="vect")
                p.wait("tensor", "vect")
                transpose_tile_to_cm(stg, CM, i, T)

    def conv_cm():
        for d in range(NC_):
            h = CM[:, T * d:T * (d + 1)]
            w0 = smallv[:, 160 + 3 * d:161 + 3 * d]
            w1 = smallv[:, 161 + 3 * d:162 + 3 * d]
            w2 = smallv[:, 162 + 3 * d:163 + 3 * d]
            u = SCR[:, 0:1024]
            for hf in (1, 0):
                b0 = 1024 * hf
                p.wait("vector", "scal")
                p.op("vector", "tensor_scalar_mul", u, h[:, b0:b0 + 1024], w2)
                if hf == 0:
                    p.op("vector", "scalar_tensor_tensor", u[:, 1:1024],
                         h[:, 0:1023], w1, u[:, 1:1024], ALU.mult, ALU.add)
                    p.op("vector", "tensor_copy", BNC[:, 0:2], u[:, 0:2])
                    p.op("vector", "scalar_tensor_tensor", BNC[:, 2:1024],
                         h[:, 0:1022], w0, u[:, 2:1024], ALU.mult, ALU.add,
                         inc="vect")
                else:
                    p.op("vector", "scalar_tensor_tensor", u,
                         h[:, b0 - 1:b0 + 1023], w1, u, ALU.mult, ALU.add)
                    p.op("vector", "scalar_tensor_tensor", BNC[:, 0:1024],
                         h[:, b0 - 2:b0 + 1022], w0, u, ALU.mult, ALU.add,
                         inc="vect")
                p.wait("scalar", "vect")
                p.op("scalar", "copy", h[:, b0:b0 + 1024], BNC[:, 0:1024],
                     inc="scal")
        p.wait("vector", "scal")
        p.wait("tensor", "scal")

    def stage_attn(XT, XF, inner):
        xf_bf = bitb(XF)
        QTd = xf_bf[:, 0:2048]
        KTd = xf_bf[:, 2048:4096]
        Vd = xf_bf[:, 4096:6272]
        EXPS = xf_bf[:, 8192:16384]
        Yb = xf_bf[:, 16384:16896]
        YTb = xf_bf[:, 16896:17408]
        if not inner:
            dma_bcast("gpsimd", M_BCA, row(bo), "qBA")
            p.wait("vector", "qBA")
            p.wait("vector", "scal")
            for i in range(NT):
                p.op("vector", "tensor_add", XT[:, 1024 * i:1024 * (i + 1)],
                     XT[:, 1024 * i:1024 * (i + 1)], M_BCA, inc="vect")
            dma_bcast("gpsimd", M_BCB, row(bv), "qBB")
            p.wait("vector", "qBB")
        for d in range(NC_):
            if inner:
                qsrc = lambda kc: attn_w_bf[128 * kc:128 * (kc + 1), 128 * d:128 * (d + 1)]
                ksrc = lambda kc: attn_w_bf[128 * kc:128 * (kc + 1), C + 128 * d:C + 128 * (d + 1)]
                vsrc = lambda kc: attn_w_bf[128 * kc:128 * (kc + 1), 2 * C + 128 * d:2 * C + 128 * (d + 1)]
            else:
                qsrc = lambda kc: wq_bf[128 * kc:128 * (kc + 1), 128 * d:128 * (d + 1)]
                ksrc = lambda kc: wk_bf[128 * kc:128 * (kc + 1), 128 * d:128 * (d + 1)]
                vsrc = lambda kc: wv_bf[128 * kc:128 * (kc + 1), 128 * d:128 * (d + 1)]
            for which, src, DST, bcol in (("q", qsrc, QTd, 0), ("k", ksrc, KTd, 8)):
                sset = 0 if which == "q" else 1
                sem = load_units((2 * sset, 2 * sset + 1),
                                 [(qk_slot(sset, kc), src(kc)) for kc in range(NC_)])
                p.wait("tensor", sem)
                for nsl in range(4):
                    pu = 2 * sset + (nsl % 2)
                    pb = phalf(sset, nsl % 2)
                    pguard.acquire(pu, "tensor")
                    for kc in range(NC_):
                        p.op("tensor", "matmul", pb,
                             qk_slot(sset, kc),
                             CM[:, T * kc + 512 * nsl: T * kc + 512 * (nsl + 1)],
                             start=(kc == 0), stop=(kc == NC_ - 1),
                             inc="tens" if kc == NC_ - 1 else None)
                    release_units((2 * sset, 2 * sset + 1), "tens")
                    p.wait("scalar", "tens")
                    if inner:
                        p.op("scalar", "copy",
                             DST[:, 512 * nsl:512 * (nsl + 1)], pb, inc="scal")
                    else:
                        p.op("scalar", "activation",
                             DST[:, 512 * nsl:512 * (nsl + 1)], pb, AF.Identity,
                             bias=smallv[:, bcol + d:bcol + d + 1], scale=1.0,
                             inc="scal")
                    pguard.release(pu, "scal")
                if inner:
                    p.wait("tensor", "scal")
                    for nsl in range(4):
                        pu = 4
                        pb = P[2][:, 0:512]
                        pguard.acquire(pu, "tensor")
                        p.op("tensor", "matmul", pb, rmt[:],
                             DST[:, 512 * nsl:512 * (nsl + 1)],
                             start=True, stop=True, inc="tens")
                        p.wait("vector", "tens")
                        s1 = SCR[:, 0:512]
                        s2 = SCR[:, 512:1024]
                        p.op("vector", "tensor_mul", s1, pb,
                             sin_dm[:, 512 * nsl:512 * (nsl + 1)])
                        pguard.release(pu, "vect")
                        p.op("vector", "tensor_mul", s2,
                             DST[:, 512 * nsl:512 * (nsl + 1)],
                             cos_dm[:, 512 * nsl:512 * (nsl + 1)])
                        p.op("vector", "tensor_add",
                             DST[:, 512 * nsl:512 * (nsl + 1)], s1, s2, inc="vect")
                        p.wait("tensor", "vect")
            if d == 0 and DBG in ("qt", "kt"):
                dump_bf(QTd if DBG == "qt" else KTd, 2048)
                return
            sem = load_units((4, 5),
                             [(v_slot(kc), vsrc(kc)) for kc in range(NC_)])
            p.wait("tensor", sem)
            p.wait("vector", "tens")
            p.op("vector", "memset", Vd[:], 0.0, inc="vect")
            p.wait("tensor", "vect")
            for tt in range(NT):
                pu = 6 + (tt % 2)
                pb = phalf(3, tt % 2)[:, 0:128]
                pguard.acquire(pu, "tensor")
                for kc in range(NC_):
                    p.op("tensor", "matmul", pb,
                         CM[:, T * kc + 128 * tt: T * kc + 128 * (tt + 1)],
                         v_slot(kc),
                         start=(kc == 0), stop=(kc == NC_ - 1),
                         inc="tens" if kc == NC_ - 1 else None)
                p.wait("vector", "tens")
                vt = Vd[:, 136 * tt:136 * (tt + 1)]
                dst3 = vt.rearrange("p (h s) -> p h s", s=VS)[:, :, 0:64]
                src3 = pb.rearrange("p (h s) -> p h s", s=64)
                if inner:
                    p.op("vector", "tensor_copy", dst3, src3, inc="vect")
                else:
                    bc3 = M_BCB[:, 128 * d:128 * (d + 1)].rearrange(
                        "p (h s) -> p h s", s=64)
                    p.op("vector", "tensor_add", dst3, src3, bc3, inc="vect")
                pguard.release(pu, "vect")
                p.op("vector", "memset",
                     vt.rearrange("p (h s) -> p h s", s=VS)[:, :, 64:65], 1.0,
                     inc="vect")
            release_units((4, 5), "tens")
            if d == 0 and DBG == "v":
                dump_bf(Vd, 2048)
                return
            for qb in range(T // QB):
                kts = range(min(NT, 2 * qb + 2)) if inner else range(NT)
                for kt in kts:
                    for hh in range(2):
                        dbase = 64 * hh
                        pu = 2 * (kt % 2) + hh
                        pb = phalf(kt % 2, hh)[:, 0:QB]
                        pguard.acquire(pu, "tensor")
                        p.op("tensor", "matmul", pb,
                             KTd[dbase:dbase + HD, 128 * kt:128 * (kt + 1)],
                             QTd[dbase:dbase + HD, QB * qb:QB * (qb + 1)],
                             start=True, stop=True,
                             inc="tens" if hh == 1 else None)
                    p.wait("scalar", "tens")
                    for hh in range(2):
                        pu = 2 * (kt % 2) + hh
                        pb = phalf(kt % 2, hh)[:, 0:QB]
                        dst = EXPS[:, (hh * NT + kt) * QB:(hh * NT + kt + 1) * QB]
                        p.op("scalar", "activation", dst, pb, AF.Exp,
                             scale=0.125, inc="scal")
                        pguard.release(pu, "scal")
                        if inner and kt >= 2 * qb:
                            rel = 128 * kt - QB * qb
                            mi = 0 if rel == 0 else 1
                            p.wait("vector", "scal")
                            p.op("vector", "tensor_mul", dst, dst,
                                 cmask[:, QB * mi:QB * (mi + 1)], inc="vect")
                p.wait("tensor", "scal")
                p.wait("tensor", "vect")
                for qt2 in range(QB // 128):
                    pvkts = range(min(NT, 2 * qb + qt2 + 1)) if inner else range(NT)
                    lkt = max(pvkts)
                    for hh in range(2):
                        pu = 4
                        pb = P[2][:, 128 * hh:128 * hh + VS]
                        pguard.acquire(pu, "tensor")
                        for kt in pvkts:
                            lhs = EXPS[:, (hh * NT + kt) * QB + 128 * qt2:
                                       (hh * NT + kt) * QB + 128 * (qt2 + 1)]
                            rhs = Vd[:, 136 * kt + VS * hh:136 * kt + VS * (hh + 1)]
                            p.op("tensor", "matmul", pb, lhs, rhs,
                                 start=(kt == 0), stop=(kt == lkt),
                                 inc="tens" if kt == lkt else None)
                        p.wait("vector", "tens")
                        p.op("vector", "reciprocal", stat[:, 380:381], pb[:, 64:65],
                             inc="vect")
                        p.wait("vector", "vect")
                        p.op("vector", "tensor_scalar_mul",
                             Yb[:, 128 * qt2 + 64 * hh:128 * qt2 + 64 * (hh + 1)],
                             pb[:, 0:64], stat[:, 380:381], inc="vect")
                        pguard.release(pu, "vect")
                if d == 0 and qb == 0 and DBG == "exps":
                    dump_bf(EXPS, 8192)
                    return
                if d == 0 and qb == 0 and DBG == "y":
                    dump_bf(Yb, 256)
                    return
                p.wait("tensor", "vect")
                tguard.acquire(0, "tensor")
                for qt2 in range(QB // 128):
                    p.op("tensor", "transpose", PT[:, 128 * qt2:128 * (qt2 + 1)],
                         Yb[:, 128 * qt2:128 * (qt2 + 1)], idb[:],
                         inc="tens" if qt2 == 1 else None)
                p.wait("vector", "tens")
                p.op("vector", "tensor_copy", YTb[:, 0:256], PT[:, 0:256],
                     inc="vect")
                tguard.release(0, "vect")
                p.wait("tensor", "vect")
                wsrc = (proj_w_bf if inner else wo_bf)
                for nsl in range(2):
                    j = (d * 2 + nsl) % 2
                    sem = load_units((6 + j,),
                                     [(op_slot(j),
                                       wsrc[128 * d:128 * (d + 1),
                                            512 * nsl:512 * (nsl + 1)])])
                    p.wait("tensor", sem)
                    for qt2 in range(QB // 128):
                        tt = (QB * qb) // 128 + qt2
                        pu = 6 + (qt2 % 2)
                        pb = phalf(3, qt2 % 2)
                        pguard.acquire(pu, "tensor")
                        p.op("tensor", "matmul", pb,
                             YTb[:, 128 * qt2:128 * (qt2 + 1)], op_slot(j),
                             start=True, stop=True, inc="tens")
                        p.wait("vector", "tens")
                        xsl = XT[:, 1024 * tt + 512 * nsl:1024 * tt + 512 * (nsl + 1)]
                        p.op("vector", "tensor_add", xsl, xsl, pb, inc="vect")
                        pguard.release(pu, "vect")
                    release_units((6 + j,), "tens")
        p.barrier()


    def park(XT):
        for i in range(NT):
            p.dma("gpsimd", park_x[128 * i:128 * (i + 1), :],
                  XT[:, 1024 * i:1024 * (i + 1)], "qP")

    unp_marks = []

    def unpark_add(XF):
        for i in range(NT):
            j = i % 2
            stg = BIG[:, 1024 * j:1024 * (j + 1)]
            if i >= 2:
                p.wait("sync", "vect", unp_marks[i - 2])
            p.dma("sync", stg, park_x[128 * i:128 * (i + 1), :], f"up{j}")
            p.wait("vector", f"up{j}")
            p.op("vector", "tensor_add", XF[:, 1024 * i:1024 * (i + 1)],
                 XF[:, 1024 * i:1024 * (i + 1)], stg, inc="vect")
            unp_marks.append(p.counts["vect"])
        p.wait("scalar", "vect")
        p.wait("tensor", "vect")
        p.wait("gpsimd", "vect")

    def stage_ffn(XT, XF, scol_base):
        ln_stats(XT, scol_base)
        p.wait("gpsimd", "qO")
        for i in range(NT):
            ln_apply(XT, i, scol_base, XF[:, 1024 * i:1024 * (i + 1)],
                     inc="scal")
        p.wait("gpsimd", "scal")
        p.wait("gpsimd", "qP")
        p.wait("gpsimd", "up0")
        p.wait("gpsimd", "up1")
        park(XT)
        dma_bcast("gpsimd", M_BCA, row(ln1_w), "qBA")
        dma_bcast("gpsimd", M_BCB, row(ln1_b), "qBB")
        p.wait("vector", "qBA")
        p.wait("vector", "qBB")
        p.wait("vector", "scal")
        make_cm(XF, scol_base + 64, wb_bcast=(M_BCA, M_BCB))
        conv_cm()
        stage_attn(XF, XT, inner=True)
        dma_bcast("gpsimd", M_BCA, row(ln2_w), "qBA")
        dma_bcast("gpsimd", M_BCB, row(ln2_b), "qBB")
        p.wait("vector", "qBA")
        p.wait("vector", "qBB")
        moe(XF, XT, scol_base + 128)
        unp_marks.clear()
        unpark_add(XF)
        p.barrier()

    def moe(XF, XT, scol):
        xt_bf = bitb(XT)
        ln_stats(XF, scol)
        for i in range(NT):
            f = SCR[:, 0:1024]
            p.wait("scalar", "tens")
            p.wait("scalar", "vect")
            ln_apply(XF, i, scol, f, inc="scal")
            p.wait("vector", "scal")
            p.op("vector", "tensor_mul", f, f, M_BCA)
            p.op("vector", "tensor_add", f, f, M_BCB, inc="vect")
            p.wait("tensor", "vect")
            tguard.acquire(0, "tensor")
            for cc in range(NC_):
                p.op("tensor", "transpose", P[3][:, 128 * cc:128 * (cc + 1)],
                     f[:, 128 * cc:128 * (cc + 1)], idf[:],
                     inc="tens" if cc == NC_ - 1 else None)
            p.wait("vector", "tens")
            dst = XT.rearrange("p (cc t) -> p cc t", t=T)[:, :, 128 * i:128 * (i + 1)]
            p.op("vector", "tensor_copy", dst,
                 P[3].rearrange("p (cc r) -> p cc r", r=128), inc="vect")
            tguard.release(0, "vect")
        for cc in range(NC_):
            p.dma("sync", rw[:, 4 * cc:4 * (cc + 1)],
                  router_w[128 * cc:128 * (cc + 1), :], "qR")
        p.wait("tensor", "qR")
        p.wait("tensor", "vect")
        pguard.acquire(4, "tensor")
        for i in range(NT):
            for cc in range(NC_):
                p.op("tensor", "matmul", P[2][:, 4 * i:4 * (i + 1)],
                     XT.rearrange("p (cc t) -> p cc t", t=T)[:, cc, 128 * i:128 * (i + 1)],
                     rw[:, 4 * cc:4 * (cc + 1)],
                     start=(cc == 0), stop=(cc == NC_ - 1),
                     inc="tens" if (cc == NC_ - 1 and i == NT - 1) else None)
        p.wait("vector", "tens")
        p.op("vector", "tensor_copy", logit[:], P[2][:, 0:64], inc="vect")
        pguard.release(4, "vect")
        L = logit.rearrange("p (t e) -> p t e", e=4)
        Me = emask.rearrange("p (t e) -> p t e", e=4)
        mx = stat[:, 368:384]
        run = stat[:, 352:368]

        def vop(*a, **k):
            p.op("vector", *a, inc="vect", **k)
            p.wait("vector", "vect")

        p.wait("vector", "vect")
        vop("tensor_max", mx, L[:, :, 0], L[:, :, 1])
        vop("tensor_max", mx, mx, L[:, :, 2])
        vop("tensor_max", mx, mx, L[:, :, 3])
        vop("memset", run, 1.0)
        for e in range(NEXP):
            ge = stat[:, 336:352]
            vop("tensor_tensor", ge, L[:, :, e], mx, ALU.is_ge)
            vop("tensor_mul", Me[:, :, e], ge, run)
            if e < NEXP - 1:
                vop("tensor_scalar", ge, ge, -1.0, 1.0, ALU.mult, ALU.add)
                vop("tensor_mul", run, run, ge)
        p.wait("vector", "tens")
        for cc in range(NC_):
            p.op("vector", "tensor_copy", CM[:, 2048 * cc:2048 * (cc + 1)],
                 XT[:, 2048 * cc:2048 * (cc + 1)],
                 inc="vect" if cc == NC_ - 1 else None)
        p.wait("tensor", "vect")
        p.wait("scalar", "vect")
        glu = xt_bf
        wv_r = exp_wv_bf.rearrange("(e c p) n -> e p c n", e=NEXP, p=128)
        pj_r = exp_proj_bf.rearrange("(e c p) n -> e p c n", e=NEXP, p=128)
        for e in range(NEXP):
            for tg in range(2):
                tok0 = 1024 * tg
                for m in range(32):
                    j = m % 2
                    gof = 2048 * j
                    gsl = load_units((4 * j, 4 * j + 1), [(
                        pool_bf[:, gof:gof + 1024].rearrange(
                            "p (c n) -> p c n", n=128),
                        wv_r[e][:, :, 128 * m:128 * (m + 1)])])
                    vsl = load_units((4 * j + 2, 4 * j + 3), [(
                        pool_bf[:, gof + 1024:gof + 2048].rearrange(
                            "p (c n) -> p c n", n=128),
                        wv_r[e][:, :, DFF + 128 * m:DFF + 128 * (m + 1)])])
                    p.wait("tensor", gsl)
                    p.wait("tensor", vsl)
                    for ns in range(2):
                        pg = phalf(0, ns)
                        pv_ = phalf(1, ns)
                        pguard.acquire(0 + ns, "tensor")
                        pguard.acquire(2 + ns, "tensor")
                        for kc in range(NC_):
                            p.op("tensor", "matmul", pg,
                                 pool_bf[:, gof + 128 * kc:gof + 128 * (kc + 1)],
                                 CM[:, T * kc + tok0 + 512 * ns:
                                    T * kc + tok0 + 512 * (ns + 1)],
                                 start=(kc == 0), stop=(kc == NC_ - 1))
                        for kc in range(NC_):
                            p.op("tensor", "matmul", pv_,
                                 pool_bf[:, gof + 1024 + 128 * kc:
                                         gof + 1024 + 128 * (kc + 1)],
                                 CM[:, T * kc + tok0 + 512 * ns:
                                    T * kc + tok0 + 512 * (ns + 1)],
                                 start=(kc == 0), stop=(kc == NC_ - 1),
                                 inc="tens" if kc == NC_ - 1 else None)
                        p.wait("scalar", "tens")
                        sg = SCR[:, 0:512]
                        p.op("scalar", "activation", sg, pg, AF.Silu, inc="scal")
                        p.wait("vector", "scal")
                        p.op("vector", "tensor_mul",
                             glu[:, 1024 * m + 512 * ns:1024 * m + 512 * (ns + 1)],
                             sg, pv_, inc="vect")
                        pguard.release(0 + ns, "vect")
                        pguard.release(2 + ns, "vect")
                    release_units((4 * j, 4 * j + 1, 4 * j + 2, 4 * j + 3), "tens")
                p.wait("tensor", "vect")
                for tgrp in range(2):
                    for nsl in range(2):
                        pus = [0, 1, 2, 3]
                        for u in pus:
                            pguard.acquire(u, "tensor")
                        for g8 in range(8):
                            j = g8 % 2
                            sem = load_units((4 * j, 4 * j + 1, 4 * j + 2, 4 * j + 3), [(
                                pool_bf[:, 2048 * j:2048 * j + 2048].rearrange(
                                    "p (c n) -> p c n", n=512),
                                pj_r[e][:, 4 * g8:4 * (g8 + 1),
                                        512 * nsl:512 * (nsl + 1)])])
                            p.wait("tensor", sem)
                            for t4 in range(4):
                                tt2 = 4 * tgrp + t4
                                pb = phalf(t4 // 2, t4 % 2)
                                for jj in range(4):
                                    kc = 4 * g8 + jj
                                    p.op("tensor", "matmul", pb,
                                         glu[:, 1024 * kc + 128 * tt2:
                                             1024 * kc + 128 * (tt2 + 1)],
                                         pool_bf[:, 2048 * j + 512 * jj:
                                                 2048 * j + 512 * (jj + 1)],
                                         start=(kc == 0), stop=(kc == 31),
                                         inc="tens" if (jj == 3 and t4 == 3)
                                         else None)
                            release_units((4 * j, 4 * j + 1, 4 * j + 2, 4 * j + 3), "tens")
                        p.wait("vector", "tens")
                        for t4 in range(4):
                            tt2 = 4 * tgrp + t4
                            tt = 8 * tg + tt2
                            pb = phalf(t4 // 2, t4 % 2)
                            p.op("vector", "scalar_tensor_tensor",
                                 XF[:, 1024 * tt + 512 * nsl:
                                    1024 * tt + 512 * (nsl + 1)],
                                 pb, emask[:, 4 * tt + e:4 * tt + e + 1],
                                 XF[:, 1024 * tt + 512 * nsl:
                                    1024 * tt + 512 * (nsl + 1)],
                                 ALU.mult, ALU.add,
                                 inc="vect" if t4 == 3 else None)
                        for u in pus:
                            pguard.release(u, "vect")

    def stage_ssm(XT, XF, scol):
        ln_stats(XT, scol)
        dma_bcast("gpsimd", M_BCA, row(D_par), "qBA")
        p.wait("vector", "qBA")
        for i in range(NT):
            f = SCR[:, 0:1024]
            p.wait("scalar", "tens")
            p.wait("scalar", "vect")
            ln_apply(XT, i, scol, f, inc="scal")
            p.wait("tensor", "scal")
            tguard.acquire(0, "tensor")
            for cc in range(NC_):
                p.op("tensor", "transpose", P[3][:, 128 * cc:128 * (cc + 1)],
                     f[:, 128 * cc:128 * (cc + 1)], idf[:],
                     inc="tens" if cc == NC_ - 1 else None)
            p.wait("vector", "scal")
            ud = BIG[:, 7168:8192]
            p.op("vector", "tensor_mul", ud, f, M_BCA)
            p.op("vector", "tensor_add", XT[:, 1024 * i:1024 * (i + 1)],
                 XT[:, 1024 * i:1024 * (i + 1)], ud)
            p.wait("vector", "tens")
            dst = XF.rearrange("p (cc t) -> p cc t", t=T)[:, :, 128 * i:128 * (i + 1)]
            p.op("vector", "tensor_copy", dst,
                 P[3].rearrange("p (cc r) -> p cc r", r=128), inc="vect")
            tguard.release(0, "vect")
        xdbl = POOL[0:96, :]
        p.wait("tensor", "vect")
        for sl4 in range(4):
            pguard.acquire(4, "tensor")
            for cc in range(NC_):
                p.op("tensor", "matmul", P[2][0:96, 0:512],
                     wxt[:, 96 * cc:96 * (cc + 1)],
                     XF[:, 2048 * cc + 512 * sl4:2048 * cc + 512 * (sl4 + 1)],
                     start=(cc == 0), stop=(cc == NC_ - 1),
                     inc="tens" if cc == NC_ - 1 else None)
            p.wait("vector", "tens")
            p.op("vector", "tensor_copy", xdbl[:, 512 * sl4:512 * (sl4 + 1)],
                 P[2][0:96, 0:512], inc="vect")
            pguard.release(4, "vect")
        p.wait("gpsimd", "vect")
        p.dma("gpsimd", bc_rows[:], xdbl[64:96, :], "qBC")
        bcg = Guard(p, 2)
        DT2 = BIG[:, 0:1024]
        DTU2 = BIG[:, 1024:2048]
        Y2 = BIG[:, 2048:3072]
        ABR2 = [BIG[:, 3072:3584], BIG[:, 5632:6144]]
        HH_ = BIG[:, 3584:4096]
        TT_ = BIG[:, 6144:6656]
        abg = Guard(p, 2)
        DBU = BIG[:, 4096:4608]
        BBC = [M_BCA, M_BCB]
        CBC = [M_SS, BIG[:, 4608:5632]]

        def bc_load(j, ng, l0):
            bcg.acquire(j, "gpsimd")
            p.dma("gpsimd", BBC[j].rearrange("p (a n) -> p a n", a=2),
                  bc_rows[2 * ng:2 * ng + 2, l0:l0 + 512].partition_broadcast(128),
                  f"bc{j}")
            p.dma("gpsimd", CBC[j].rearrange("p (a n) -> p a n", a=2),
                  bc_rows[16 + 2 * ng:18 + 2 * ng, l0:l0 + 512].partition_broadcast(128),
                  f"bc{j}")

        for lq in range(4):
            l0 = 512 * lq
            for dp in range(4):
                for dh in range(2):
                    d = 2 * dp + dh
                    pguard.acquire(4, "tensor")
                    p.op("tensor", "matmul", P[2][:, 0:512],
                         wdtt[:, 128 * d:128 * (d + 1)],
                         xdbl[0:64, l0:l0 + 512],
                         start=True, stop=True, inc="tens")
                    bdt = smallv[:, 16 + d:17 + d]
                    p.wait("scalar", "tens")
                    p.wait("scalar", "vect")
                    dsl = DT2[:, 512 * dh:512 * (dh + 1)]
                    p.op("scalar", "activation", dsl, P[2][:, 0:512],
                         AF.Relu, bias=bdt, scale=1.0)
                    p.op("scalar", "activation", SCR[:, 0:512], P[2][:, 0:512],
                         AF.Abs, bias=bdt, scale=1.0)
                    p.op("scalar", "activation", SCR[:, 0:512], SCR[:, 0:512],
                         AF.Exp, scale=-1.0)
                    p.op("scalar", "activation", SCR[:, 0:512], SCR[:, 0:512],
                         AF.Ln, bias=1.0, inc="scal")
                    pguard.release(4, "scal")
                    p.wait("vector", "scal")
                    p.op("vector", "tensor_add", dsl, dsl, SCR[:, 0:512])
                    p.op("vector", "tensor_mul", DTU2[:, 512 * dh:512 * (dh + 1)],
                         dsl, XF[:, 2048 * d + l0:2048 * d + l0 + 512], inc="vect")
                p.wait("gpsimd", "qBC")
                bc_load(0, 0, l0)
                for ng in range(8):
                    j = ng % 2
                    if ng < 7:
                        bc_load(1 - j, ng + 1, l0)
                    p.wait("vector", f"bc{j}")
                    for nh in range(2):
                        n = 2 * ng + nh
                        bb = BBC[j][:, 512 * nh:512 * (nh + 1)]
                        cb = CBC[j][:, 512 * nh:512 * (nh + 1)]
                        for dh in range(2):
                            d = 2 * dp + dh
                            it = dh
                            dsl = DT2[:, 512 * dh:512 * (dh + 1)]
                            A_col = smallv[:, 24 + 16 * d + n:25 + 16 * d + n]
                            abg.acquire(it, "scalar")
                            p.op("scalar", "activation", ABR2[it], dsl, AF.Exp,
                                 scale=A_col, inc="scal")
                            p.wait("vector", "scal")
                            p.op("vector", "tensor_mul", DBU,
                                 DTU2[:, 512 * dh:512 * (dh + 1)], bb)
                            p.op("vector", "tensor_tensor_scan", HH_, ABR2[it], DBU,
                                 0.0 if lq == 0 else carry[:, 16 * d + n:16 * d + n + 1],
                                 ALU.mult, ALU.add, inc="vect")
                            abg.release(it, "vect")
                            p.wait("vector", "vect")
                            if lq < 3:
                                p.op("vector", "tensor_copy",
                                     carry[:, 16 * d + n:16 * d + n + 1],
                                     HH_[:, 511:512])
                            p.op("vector", "tensor_mul", TT_, HH_, cb, inc="vect")
                            ysl = Y2[:, 512 * dh:512 * (dh + 1)]
                            if n == 0:
                                p.op("vector", "tensor_copy", ysl, TT_, inc="vect")
                            else:
                                p.op("vector", "tensor_add", ysl, ysl, TT_,
                                     inc="vect")
                    bcg.release(j, "vect")
                for dh in range(2):
                    d = 2 * dp + dh
                    p.wait("tensor", "vect")
                    tguard.acquire(0, "tensor")
                    for jj in range(4):
                        p.op("tensor", "transpose", P[3][:, 128 * jj:128 * (jj + 1)],
                             Y2[:, 512 * dh + 128 * jj:512 * dh + 128 * (jj + 1)],
                             idf[:], inc="tens" if jj == 3 else None)
                    p.wait("vector", "tens")
                    dstx = XT.rearrange("p (tt t) -> p tt t", t=1024)[
                        :, 4 * lq:4 * lq + 4, 128 * d:128 * (d + 1)]
                    p.op("vector", "tensor_add", dstx, dstx,
                         P[3][:, 0:512].rearrange("p (j r) -> p j r", r=128),
                         inc="vect")
                    tguard.release(0, "vect")
        p.barrier()
    with es:
        load_consts()
        prologue_weights()
        bufs = [XA, XB]
        for b in range(BPC):
            load_x(b, bufs[0])
            for e in ("vector", "scalar", "tensor", "gpsimd"):
                p.wait(e, "qX")
            if DBG == "xa" and b == 0:
                dumped[0] = True
                p.barrier()
                for i in range(NT):
                    p.dma("gpsimd", x_out[128 * i:128 * (i + 1), :],
                          bufs[0][:, 1024 * i:1024 * (i + 1)], "qO")
                p.wait("gpsimd", "qO")
            if DBG in ("wq", "wq16") and b == 0:
                p.barrier()
                for r in range(NC_):
                    p.dma("sync", big_bf[:, 1024 * r:1024 * (r + 1)],
                          wq_bf[128 * r:128 * (r + 1), :], "qX")
                if DBG == "wq16":
                    for r in range(NC_):
                        p.dma("sync", big_bf[:, 8192 + 1024 * r:8192 + 1024 * (r + 1)],
                              wk_bf[128 * r:128 * (r + 1), :], "qX")
                p.wait("scalar", "qX")
                dump_bf(big_bf[:, 0:8192] if DBG == "wq" else big_bf[:, 0:16384],
                        8192 if DBG == "wq" else 16384)
            xcur, xfree = bufs[0], bufs[1]
            if STAGES >= 1:
                make_cm(xcur, 0)
                if DBG == "cm" and b == 0:
                    dump_bf(CM, NC_ * T)
                if not dumped[0]:
                    stage_attn(xcur, xfree, inner=False)
            if STAGES >= 2 and not dumped[0]:
                stage_ffn(xcur, xfree, 64)
                xcur, xfree = xfree, xcur
            if STAGES >= 3 and not dumped[0]:
                stage_ssm(xcur, xfree, 0)
            if STAGES >= 4 and not dumped[0]:
                stage_ffn(xcur, xfree, 64)
                xcur, xfree = xfree, xcur
            if not dumped[0]:
                store_x(b, xcur)
            if dumped[0]:
                break
        p.wait("gpsimd", "qO")
        p.barrier()

        with nc.Block() as block:
            for ename in ENGINES:
                def make(en):
                    def _(eng):
                        for fn in p.streams[en]:
                            fn(eng)
                    return _
                getattr(block, ename)(make(ename))
    return nc


def _constants():
    ident = np.eye(128, dtype=np.float32)
    ident_b = np.eye(128).astype(ml_dtypes.bfloat16)
    inv_freq = 1.0 / (10000.0 ** (np.arange(0, HD, 2, dtype=np.float64) / HD))
    t = np.arange(T, dtype=np.float64)
    freqs = np.einsum("i,j->ij", t, inv_freq)
    emb = np.concatenate([freqs, freqs], axis=-1)
    cos = np.cos(emb)
    sin = np.sin(emb)
    cos_dm = np.concatenate([cos.T, cos.T], axis=0).astype(ml_dtypes.bfloat16)
    sin_dm = np.concatenate([sin.T, sin.T], axis=0).astype(ml_dtypes.bfloat16)
    R = np.zeros((128, 128), dtype=np.float64)
    for hh in range(2):
        o = 64 * hh
        for dd in range(32):
            R[o + dd, o + dd + 32] = -1.0
            R[o + dd + 32, o + dd] = 1.0
    rmat = R.T.astype(ml_dtypes.bfloat16)
    kk = np.arange(128)[:, None]
    qq = np.arange(QB)[None, :]
    masks = np.concatenate(
        [((rel + kk) <= qq).astype(np.float64) for rel in (0, 128)],
        axis=1).astype(ml_dtypes.bfloat16)
    return ident, ident_b, cos_dm, sin_dm, rmat, masks


def kernel(**inputs):
    g = lambda k: np.ascontiguousarray(np.asarray(inputs[k], dtype=np.float32))
    x = g("L0_attn_a_x")
    ident, ident_b, cos_dm, sin_dm, rmat, masks = _constants()
    base = dict(
        Wq=g("L0_attn_a_Wq"), bq=g("L0_attn_a_bq"), Wk=g("L0_attn_a_Wk"),
        bk=g("L0_attn_a_bk"), Wv=g("L0_attn_a_Wv"), bv=g("L0_attn_a_bv"),
        Wo=g("L0_attn_a_Wo"), bo=g("L0_attn_a_bo"),
        A_log=g("L0_attn_b_A_log"), W_x=g("L0_attn_b_W_x"),
        W_dt=g("L0_attn_b_W_dt"), b_dt=g("L0_attn_b_b_dt"),
        D_par=g("L0_attn_b_D_param"),
        ln1_w=g("L0_ffn_ln1_w"), ln1_b=g("L0_ffn_ln1_b"),
        ln2_w=g("L0_ffn_ln2_w"), ln2_b=g("L0_ffn_ln2_b"),
        conv_w=g("L0_ffn_conv_w").reshape(C, 3),
        attn_w=g("L0_ffn_attn_w"), proj_w=g("L0_ffn_proj_w"),
        router_w=g("L0_ffn_router_w"),
        exp_wv=g("L0_ffn_expert_wv").reshape(NEXP * C, 2 * DFF),
        exp_proj=g("L0_ffn_expert_proj").reshape(NEXP * DFF, C),
        ident_f=ident, ident_b=ident_b, rmat=rmat,
        cos_dm_i=cos_dm, sin_dm_i=sin_dm, masks=masks,
    )
    in_maps = []
    for c in range(NCORES):
        m = dict(base)
        m["x_in"] = np.ascontiguousarray(
            x[BPC * c:BPC * (c + 1)].reshape(BPC * T, C))
        in_maps.append(m)
    nc = build_program()
    res = run_bass_kernel_spmd(nc, in_maps, core_ids=list(range(NCORES)))
    out = np.concatenate(
        [r["x_out"].reshape(BPC, T, C) for r in res.results], axis=0)
    return out.astype(np.float32)


